# version 03
# baseline (speedup 1.0000x reference)
import numpy as np
import concourse.bass as bass
import concourse.mybir as mybir
from concourse.bass_utils import run_bass_kernel_spmd
from concourse.alu_op_type import AluOpType

F32 = mybir.dt.float32
BF16 = mybir.dt.bfloat16
I32 = mybir.dt.int32
AF = mybir.ActivationFunctionType
AX = mybir.AxisListType

B_FULL, S, DM = 16, 2048, 1024
NCORES = 8
B_CORE = B_FULL // NCORES
H, HD = 16, 64
E, FF = 8, 2048
NTC = S // 128
NDC = DM // 128
NFC = FF // 128

G1, LA1, G2, PA, PB, PC, PBIAS = (9.33208063, -6.83677266, 0.31989422,
                                  -16.53699443, 30.24374182, 170.18486467,
                                  -249.23997529)
SSHIFT = 40.0

A_QT = 0
A_KT = 16384
A_VT = 32768
A_R4 = 49152
A_SP = 73728
AR_COLS = 81920

ST0 = A_SP
ST1 = A_SP + 2048
BIAS512 = A_SP + 4096
QHAT = A_SP + 4608
SILU = A_SP + 5120
COLV = A_SP + 6144
RTR = A_SP + 6400
FUSE = A_R4 + 16384

FA_COLS = 3072


class SB:

    ENGS = ("pe", "act", "dve", "pool", "sync")

    def __init__(self, nc):
        self.nc = nc
        self.h = {"pe": nc.tensor, "act": nc.scalar, "dve": nc.vector,
                  "pool": nc.gpsimd, "sync": nc.sync}
        self.sem = {e: nc.alloc_semaphore(name=f"sb_{e}") for e in self.ENGS}
        self.cnt = {e: 0 for e in self.ENGS}
        self.sat = {c: {p: 0 for p in self.ENGS} for c in self.ENGS}
        self.res = {}

    def barrier(self):
        for c in self.ENGS:
            h = self.h[c]
            for p in self.ENGS:
                if p == c or self.cnt[p] == 0:
                    continue
                if self.cnt[p] > self.sat[c][p]:
                    h.wait_ge(self.sem[p], self.cnt[p])
                    self.sat[c][p] = self.cnt[p]

    def emit(self, engine, fn, reads=(), writes=(), dma=False):
        deps = {}

        def dep(eng, cntv):
            if cntv > deps.get(eng, 0):
                deps[eng] = cntv

        for r in reads:
            e = self.res.get(r)
            if e and e["w"]:
                dep(*e["w"])
        for w in writes:
            e = self.res.get(w)
            if e:
                if e["w"]:
                    dep(*e["w"])
                for pr in e["r"]:
                    dep(*pr)
        hh = self.h[engine]
        for peng in self.ENGS:
            pcnt = deps.get(peng, 0)
            if pcnt > self.sat[engine][peng]:
                hh.wait_ge(self.sem[peng], pcnt)
                self.sat[engine][peng] = pcnt
        inst = fn(hh)
        inst.then_inc(self.sem[engine], 16 if dma else 1)
        self.cnt[engine] += 16 if dma else 1
        my = (engine, self.cnt[engine])
        for r in reads:
            self.res.setdefault(r, {"w": None, "r": []})["r"].append(my)
        for w in writes:
            self.res[w] = {"w": my, "r": []}
        return inst


def build_program(debug=0, bmask=0):
    nc = bass.Bass("TRN2", target_bir_lowering=False)

    x_d = nc.dram_tensor("x", [B_CORE, S, DM], F32, kind="ExternalInput").ap()
    qkvw_d = nc.dram_tensor("qkv_w", [3 * DM, DM], F32, kind="ExternalInput").ap()
    qkvb_d = nc.dram_tensor("qkv_b", [3 * DM], F32, kind="ExternalInput").ap()
    outw_d = nc.dram_tensor("out_w", [DM, DM], F32, kind="ExternalInput").ap()
    outb_d = nc.dram_tensor("out_b", [DM], F32, kind="ExternalInput").ap()
    rtr_d = nc.dram_tensor("router", [DM, E], F32, kind="ExternalInput").ap()
    wg_d = nc.dram_tensor("w_gate", [E * DM, FF], F32, kind="ExternalInput").ap()
    wu_d = nc.dram_tensor("w_up", [E * DM, FF], F32, kind="ExternalInput").ap()
    wd_d = nc.dram_tensor("w_down", [E * FF, DM], F32, kind="ExternalInput").ap()
    out_d = nc.dram_tensor("out", [B_CORE, S, DM], F32, kind="ExternalOutput").ap()
    x2 = x_d.rearrange("b s d -> (b s) d")
    o2 = out_d.rearrange("b s d -> (b s) d")

    dbg = {}
    if debug:
        for nm, shp in [("q", [NDC, 128, S]), ("k", [NDC, 128, S]),
                        ("v", [NTC, 128, DM]), ("attnT", [NDC, 128, S]),
                        ("x1", [NDC, 128, S]), ("wfull", [NTC, 128, E]),
                        ("y", [NTC, 128, DM]),
                        ("st", [NTC, 128, S]), ("cs", [NTC, 128, 2]),
                        ("u", [1, 128, S]), ("svout", [1, 128, S]),
                        ("dena", [1, 128, S]), ("colv", [NTC, 128, 65]), ("colx", [NTC, 128, 4])]:
            dbg[nm] = nc.dram_tensor("dbg_" + nm, shp, F32, kind="ExternalOutput").ap()

    AR = nc.alloc_sbuf_tensor("AR", [128, AR_COLS], BF16).ap()
    FA = nc.alloc_sbuf_tensor("FA", [128, FA_COLS], F32).ap()
    XT = nc.alloc_sbuf_tensor("XT", [128, NDC * S], BF16).ap()
    id32 = nc.alloc_sbuf_tensor("id32", [128, 128], F32).ap()
    id16 = nc.alloc_sbuf_tensor("id16", [128, 128], BF16).ap()
    io_a = nc.alloc_sbuf_tensor("io_a", [128, 128], I32).ap()
    io_b = nc.alloc_sbuf_tensor("io_b", [128, 128], I32).ap()
    ones_col16 = nc.alloc_sbuf_tensor("ones_col16", [128, 1], BF16).ap()
    ones_row32 = nc.alloc_sbuf_tensor("ones_row32", [1, 128], F32).ap()
    ones64 = nc.alloc_sbuf_tensor("ones64", [65, 64], F32).ap()
    bvec = nc.alloc_sbuf_tensor("bvec", [128, 8], F32).ap()
    outb_sb = nc.alloc_sbuf_tensor("outb_sb", [128, NDC], F32).ap()
    wfull = nc.alloc_sbuf_tensor("wfull", [128, NTC * E], F32).ap()
    rt8 = nc.alloc_sbuf_tensor("rt8", [128, 8], F32).ap()
    rt1 = nc.alloc_sbuf_tensor("rt1", [128, 8], F32).ap()
    rt2 = nc.alloc_sbuf_tensor("rt2", [128, 8], F32).ap()
    rtc = nc.alloc_sbuf_tensor("rtc", [128, 6], F32).ap()
    ss8 = nc.alloc_sbuf_tensor("ss8", [128, 16], F32).ap()
    cs2 = nc.alloc_sbuf_tensor("cs2", [128, 4], F32).ap()
    csx = nc.alloc_sbuf_tensor("csx", [128, 4], F32).ap()
    stat = nc.alloc_sbuf_tensor("stat", [128, 16], F32).ap()

    psA = nc.alloc_psum_tensor("psA", [128, 2048], F32).ap()
    psB = nc.alloc_psum_tensor("psB", [128, 2048], F32).ap()

    sb = SB(nc)

    def arb(col, width):
        return [("ar", b) for b in range(col // 2048, (col + width + 2047) // 2048)]

    def fab(col, width):
        return [("fa", b) for b in range(col // 512, (col + width + 511) // 512)]

    def psq(t, col, width):
        return [(t, b) for b in range(col // 512, (col + width + 511) // 512)]

    PSA, PSB = "psA", "psB"
    xtr = lambda dc, blk: ("xt", dc, blk)

    sb.emit("pool", lambda g: g.iota(io_a, [[1, 128]], base=0, channel_multiplier=0),
            writes=[("io", 0)])
    sb.emit("pool", lambda g: g.iota(io_b, [[0, 128]], base=0, channel_multiplier=1),
            writes=[("io", 1)])
    sb.emit("dve", lambda v: v.tensor_tensor(id32, io_b, io_a, op=AluOpType.is_equal),
            reads=[("io", 0), ("io", 1)], writes=[("id32",)])
    sb.emit("dve", lambda v: v.tensor_copy(id16, id32),
            reads=[("id32",)], writes=[("id16",)])
    sb.emit("dve", lambda v: v.memset(ones_col16, 1.0), writes=[("oc16",)])
    sb.emit("dve", lambda v: v.memset(ones_row32, 1.0), writes=[("or32",)])
    sb.emit("dve", lambda v: v.memset(ones64, 1.0), writes=[("o64",)])
    for i, val in [(0, PBIAS + SSHIFT), (1, LA1), (2, 1e-5), (3, 1e-20), (4, PB)]:
        sb.emit("dve", lambda v, i=i, val=val: v.memset(bvec[:, i:i + 1], val),
                writes=[("bv", i)])
    sb.emit("sync", lambda q: q.dma_start(
        outb_sb, outb_d.rearrange("(c p) -> p c", p=128),
        allow_slow_non_contiguous=True), writes=[("outb",)], dma=True)
    for dc in range(NDC):
        sb.emit("sync", lambda q, dc=dc: q.dma_start(
            FA[:, dc * 8:dc * 8 + 8], rtr_d[dc * 128:(dc + 1) * 128, :]),
            writes=fab(dc * 8, 8), dma=True)
    sb.emit("pool", lambda g: g.tensor_copy(AR[:, RTR:RTR + 64], FA[:, 0:64]),
            reads=fab(0, 64), writes=arb(RTR, 64))

    def load_xt(s):
        for tc in range(NTC):
            stg = (tc % 2) * 1024
            sb.emit("sync", lambda q, tc=tc, stg=stg: q.dma_start(
                FA[:, stg:stg + 1024], x2[s * S + tc * 128: s * S + (tc + 1) * 128, :]),
                writes=fab(stg, 1024), dma=True)
            for dc in range(NDC):
                pcol = (dc % 2) * 128
                sb.emit("pe", lambda t, dc=dc, stg=stg, pcol=pcol: t.transpose(
                    psB[:, pcol:pcol + 128], FA[:, stg + dc * 128:stg + dc * 128 + 128], id32),
                    reads=fab(stg + dc * 128, 128) + [("id32",)],
                    writes=psq(PSB, pcol, 128))
                sb.emit("dve", lambda v, dc=dc, tc=tc, pcol=pcol: v.tensor_copy(
                    XT[:, dc * S + tc * 128: dc * S + tc * 128 + 128], psB[:, pcol:pcol + 128]),
                    reads=psq(PSB, pcol, 128), writes=[xtr(dc, tc // 4), ("xtf", dc, tc)])

    def load_transpose_w(dram, rows, wcol_base, chunk_cols):
        for rc in range(rows // 128):
            stg = 2048
            sb.emit("sync", lambda q, rc=rc: q.dma_start(
                FA[:, stg:stg + 1024], dram[rc * 128:(rc + 1) * 128, :]),
                writes=fab(stg, 1024), dma=True)
            for dc in range(NDC):
                pcol = 256 + (dc % 2) * 128
                sb.emit("pe", lambda t, dc=dc, pcol=pcol: t.transpose(
                    psB[:, pcol:pcol + 128], FA[:, stg + dc * 128:stg + dc * 128 + 128], id32),
                    reads=fab(stg + dc * 128, 128) + [("id32",)],
                    writes=psq(PSB, pcol, 128))
                dst = wcol_base + dc * chunk_cols + rc * 128
                sb.emit("dve", lambda v, dst=dst, pcol=pcol: v.tensor_copy(
                    AR[:, dst:dst + 128], psB[:, pcol:pcol + 128]),
                    reads=psq(PSB, pcol, 128), writes=arb(dst, 128))

    def qkv_phase(s):
        load_transpose_w(qkvw_d, 3 * DM, A_R4, 3072)
        wqkv_res = arb(A_R4, 24576)
        for fch in range(6):
            fbase = fch * 512
            sb.emit("sync", lambda q, fbase=fbase: q.dma_start(
                FA[0:1, 1536:2048], qkvb_d[fbase:fbase + 512].rearrange("(o n) -> o n", o=1)),
                writes=[("qbrow",)], dma=True)
            sb.emit("pe", lambda t: t.matmul(
                psB[:, 1536:2048], ones_row32, FA[0:1, 1536:2048], start=True, stop=True),
                reads=[("qbrow",), ("or32",)], writes=psq(PSB, 1536, 512))
            sb.emit("dve", lambda v: v.tensor_copy(FA[:, 1024:1536], psB[:, 1536:2048]),
                    reads=psq(PSB, 1536, 512), writes=fab(1024, 512))
            for tc in range(NTC):
                pq = (tc % 2) * 512

                def mmg(t, tc=tc, fbase=fbase, pq=pq):
                    for dc in range(NDC):
                        i = t.matmul(psA[:, pq:pq + 512],
                                     XT[:, dc * S + tc * 128: dc * S + tc * 128 + 128],
                                     AR[:, A_R4 + dc * 3072 + fbase: A_R4 + dc * 3072 + fbase + 512],
                                     start=(dc == 0), stop=(dc == NDC - 1))
                    return i
                sb.emit("pe", mmg,
                        reads=[("xtf", dc, tc) for dc in range(NDC)] + wqkv_res,
                        writes=psq(PSA, pq, 512))
                sb.emit("dve", lambda v, pq=pq: v.tensor_tensor(
                    FA[:, 0:512], psA[:, pq:pq + 512], FA[:, 1024:1536], op=AluOpType.add),
                    reads=psq(PSA, pq, 512) + fab(1024, 512), writes=fab(0, 512))
                if fch >= 4:
                    vcol = A_VT + tc * 1024 + (fch - 4) * 512
                    sb.emit("dve", lambda v, vcol=vcol: v.tensor_copy(
                        AR[:, vcol:vcol + 512], FA[:, 0:512]),
                        reads=fab(0, 512), writes=arb(vcol, 512))
                else:
                    for hh in range(8):
                        sb.emit("dve", lambda v, hh=hh: v.scalar_tensor_tensor(
                            FA[:, 512 + hh * 64:512 + hh * 64 + 64],
                            FA[:, hh * 64:hh * 64 + 64], 1.0,
                            FA[:, hh * 64:hh * 64 + 64],
                            op0=AluOpType.mult, op1=AluOpType.mult,
                            accum_out=ss8[:, hh:hh + 1]),
                            reads=fab(0, 512), writes=fab(512, 64) + [("ss", hh)])
                    sb.emit("act", lambda a: a.activation(
                        ss8[:, 8:16], ss8[:, 0:8], AF.Ln, bias=bvec[:, 3:4], scale=1.0),
                        reads=[("ss", hh) for hh in range(8)] + [("bv", 3)],
                        writes=[("ssl",)])
                    sb.emit("act", lambda a: a.activation(
                        ss8[:, 0:8], ss8[:, 8:16], AF.Exp, bias=0.0, scale=-0.5),
                        reads=[("ssl",)], writes=[("rn",)] + [("ss", hh) for hh in range(8)])
                    for hh in range(8):
                        sb.emit("dve", lambda v, hh=hh: v.tensor_scalar(
                            FA[:, 512 + hh * 64:512 + hh * 64 + 64],
                            FA[:, hh * 64:hh * 64 + 64],
                            ss8[:, hh:hh + 1], None, op0=AluOpType.mult),
                            reads=fab(0, 512) + [("rn",)] + fab(512 + hh * 64, 64),
                            writes=[("qhf", hh)])
                    base_region = A_QT if fch < 2 else A_KT
                    for pp in range(4):
                        chunk = (fch % 2) * 4 + pp
                        pcol = 1024 + (pp % 4) * 128
                        sb.emit("pe", lambda t, pp=pp, pcol=pcol: t.transpose(
                            psA[:, pcol:pcol + 128],
                            FA[:, 512 + pp * 128:512 + pp * 128 + 128], id32),
                            reads=[("qhf", 2 * pp), ("qhf", 2 * pp + 1), ("id32",)],
                            writes=psq(PSA, pcol, 128))
                        dst = base_region + chunk * 2048 + tc * 128
                        sb.emit("dve", lambda v, dst=dst, pcol=pcol: v.tensor_copy(
                            AR[:, dst:dst + 128], psA[:, pcol:pcol + 128]),
                            reads=psq(PSA, pcol, 128), writes=arb(dst, 128))

    def attn_phase(s):
        for head in range(H):
            chunk, prow = head // 2, (head % 2) * 64
            qt_res = arb(A_QT + chunk * 2048, 2048)
            kt_res = arb(A_KT + chunk * 2048, 2048)
            for kc in range(NTC):
                stc = ST0 if kc % 2 == 0 else ST1
                for q4 in range(4):
                    par = ((kc * 4 + q4) % 2) * 1536
                    pq = q4 * 512
                    sb.emit("pe", lambda t, kc=kc, chunk=chunk, prow=prow, pq=pq: t.matmul(
                        psA[:, pq:pq + 512],
                        AR[prow:prow + 64,
                           A_KT + chunk * 2048 + kc * 128: A_KT + chunk * 2048 + kc * 128 + 128],
                        AR[prow:prow + 64,
                           A_QT + chunk * 2048 + pq: A_QT + chunk * 2048 + pq + 512],
                        start=True, stop=True),
                        reads=qt_res + kt_res, writes=psq(PSA, pq, 512))
                    sb.emit("act", lambda a, par=par, pq=pq: a.activation(
                        FA[:, par:par + 512], psA[:, pq:pq + 512], AF.Exp,
                        bias=bvec[:, 1:2], scale=G1),
                        reads=psq(PSA, pq, 512) + [("bv", 1)], writes=fab(par, 512))
                    sb.emit("act", lambda a, par=par, pq=pq: a.activation(
                        FA[:, par + 512:par + 1024], psA[:, pq:pq + 512], AF.Exp,
                        bias=0.0, scale=G2),
                        reads=psq(PSA, pq, 512), writes=fab(par + 512, 512))
                    sb.emit("act", lambda a, par=par: a.activation(
                        FA[:, par + 1024:par + 1536], FA[:, par + 512:par + 1024],
                        AF.Square, bias=bvec[:, 4:5], scale=PA),
                        reads=fab(par + 512, 512) + [("bv", 4)],
                        writes=fab(par + 1024, 512))
                    sb.emit("pool", lambda g, par=par: g.tensor_tensor(
                        FA[:, par + 512:par + 1024], FA[:, par:par + 512],
                        FA[:, par + 1024:par + 1536], op=AluOpType.add),
                        reads=fab(par, 512) + fab(par + 1024, 512),
                        writes=fab(par + 512, 512))
                    sb.emit("dve", lambda v, par=par, pq=pq: v.scalar_tensor_tensor(
                        FA[:, par:par + 512], psA[:, pq:pq + 512], PC,
                        FA[:, par + 512:par + 1024],
                        op0=AluOpType.mult, op1=AluOpType.add),
                        reads=psq(PSA, pq, 512) + fab(par + 512, 512),
                        writes=fab(par, 512))
                    sb.emit("act", lambda a, par=par, stc=stc, q4=q4: a.activation(
                        AR[:, stc + q4 * 512:stc + q4 * 512 + 512], FA[:, par:par + 512],
                        AF.Exp, bias=bvec[:, 0:1], scale=1.0,
                        accum_out=cs2[:, q4:q4 + 1]),
                        reads=fab(par, 512) + [("bv", 0)],
                        writes=arb(stc + q4 * 512, 512) + [("cs", q4)])
                sb.emit("act", lambda a: a.activation(
                    ss8[:, 12:16], cs2[:, 0:4], AF.Copy, bias=0.0, scale=1.0,
                    accum_out=csx[:, 0:1]),
                    reads=[("cs", i) for i in range(4)], writes=[("css",), ("cssj",)])
                sb.emit("act", lambda a: a.activation(
                    csx[:, 1:2], csx[:, 0:1], AF.Ln, bias=0.0, scale=1.0),
                    reads=[("css",)], writes=[("csl",)])
                sb.emit("act", lambda a: a.activation(
                    csx[:, 2:3], csx[:, 1:2], AF.Exp, bias=0.0, scale=-0.5),
                    reads=[("csl",)], writes=[("col",)])
                if debug == 31 and head == 0:
                    sb.emit("sync", lambda q, kc=kc: q.dma_start(
                        dbg["colx"][kc], csx[:, 0:4]),
                        reads=[("col",), ("css",), ("csl",)], writes=[("dmp",)], dma=True)
                cvc = COLV + (kc % 2) * 65
                sb.emit("dve", lambda v, kc=kc, head=head, cvc=cvc: v.tensor_scalar(
                    AR[:, cvc:cvc + 64],
                    AR[:, A_VT + kc * 1024 + head * 64: A_VT + kc * 1024 + head * 64 + 64],
                    csx[:, 2:3], None, op0=AluOpType.mult),
                    reads=[("col",)] + arb(A_VT + kc * 1024 + head * 64, 64),
                    writes=[("colv", kc % 2)])
                sb.emit("dve", lambda v, cvc=cvc: v.tensor_copy(
                    AR[:, cvc + 64:cvc + 65], csx[:, 2:3]),
                    reads=[("col",)], writes=[("colvd", kc % 2)])

                def mmsv(t, kc=kc, stc=stc, cvc=cvc):
                    for qc in range(4):
                        i = t.matmul(psB[0:65, qc * 512:(qc + 1) * 512],
                                     AR[:, cvc:cvc + 65],
                                     AR[:, stc + qc * 512:stc + qc * 512 + 512],
                                     start=(kc == 0), stop=(kc == NTC - 1))
                    return i
                if debug == 31 and head == 0:
                    if kc == 0:
                        sb.emit("dve", lambda v: v.tensor_copy(FA[:, 1024:3072], psA[:, 0:2048]),
                                reads=psq(PSA, 0, 2048), writes=fab(1024, 2048))
                        sb.emit("sync", lambda q: q.dma_start(dbg["u"][0], FA[:, 1024:3072]),
                                reads=fab(1024, 2048), writes=[("dmp",)], dma=True)
                    sb.emit("dve", lambda v, stc=stc: v.tensor_copy(
                        FA[:, 1024:3072], AR[:, stc:stc + 2048]),
                        reads=arb(stc, 2048), writes=fab(1024, 2048))
                    sb.emit("sync", lambda q, kc=kc: q.dma_start(
                        dbg["st"][kc], FA[:, 1024:3072]),
                        reads=fab(1024, 2048), writes=[("dmp",)], dma=True)
                    sb.emit("sync", lambda q, kc=kc: q.dma_start(
                        dbg["cs"][kc], cs2[:, 0:2]),
                        reads=[("cs", 0), ("cs", 1)], writes=[("dmp",)], dma=True)
                if debug == 31 and head == 0:
                    sb.emit("dve", lambda v, cvc=cvc: v.tensor_copy(
                        FA[:, 0:65], AR[:, cvc:cvc + 65]),
                        reads=[("colv", kc % 2), ("colvd", kc % 2)], writes=fab(0, 65))
                    sb.emit("sync", lambda q, kc=kc: q.dma_start(
                        dbg["colv"][kc], FA[:, 0:65]),
                        reads=fab(0, 65), writes=[("dmp",)], dma=True)
                sb.emit("pe", mmsv,
                        reads=[("colv", kc % 2), ("colvd", kc % 2)] + arb(stc, 2048),
                        writes=psq(PSB, 0, 2048))
            if debug == 31 and head == 0:
                sb.emit("dve", lambda v: v.tensor_copy(FA[:, 1024:3072], psB[:, 0:2048]),
                        reads=psq(PSB, 0, 2048), writes=fab(1024, 2048))
                sb.emit("sync", lambda q: q.dma_start(dbg["svout"][0], FA[:, 1024:3072]),
                        reads=fab(1024, 2048), writes=[("dmp",)], dma=True)
            sb.emit("dve", lambda v: v.reciprocal(FA[64:65, 1024:3072], psB[64:65, 0:2048]),
                    reads=psq(PSB, 0, 2048) + fab(1024, 2048), writes=[("denr",)])
            if debug == 31 and head == 0:
                sb.emit("sync", lambda q: q.dma_start(dbg["dena"][0], FA[:, 1024:3072]),
                        reads=[("denr",)] + fab(1024, 2048), writes=[("dmp",)], dma=True)
            for qc in range(4):
                pcol = 1024 + (qc % 2) * 512
                fcol = (qc % 2) * 512
                sb.emit("pe", lambda t, qc=qc, pcol=pcol: t.matmul(
                    psA[0:64, pcol:pcol + 512], ones64[64:65, :],
                    FA[64:65, 1024 + qc * 512:1024 + qc * 512 + 512], start=True, stop=True),
                    reads=[("denr",), ("o64",)], writes=psq(PSA, pcol, 512))
                sb.emit("dve", lambda v, pcol=pcol, fcol=fcol: v.tensor_copy(
                    FA[0:64, fcol:fcol + 512], psA[0:64, pcol:pcol + 512]),
                    reads=psq(PSA, pcol, 512), writes=[("dbc", fcol)] + fab(fcol, 512))
                dst = A_R4 + chunk * 2048 + qc * 512
                sb.emit("dve", lambda v, qc=qc, dst=dst, prow=prow, fcol=fcol: v.tensor_tensor(
                    AR[prow:prow + 64, dst:dst + 512], psB[0:64, qc * 512:qc * 512 + 512],
                    FA[0:64, fcol:fcol + 512], op=AluOpType.mult),
                    reads=psq(PSB, qc * 512, 512) + [("dbc", fcol)] + fab(fcol, 512),
                    writes=arb(dst, 512))

    def outproj_ln1(s):
        load_transpose_w(outw_d, DM, A_QT, 1024)
        wout_res = arb(A_QT, 8192)
        at_res = arb(A_R4, 16384)
        for dcp in range(NDC):
            for qc in range(4):
                pq = (qc % 2) * 512

                def mmo(t, dcp=dcp, qc=qc, pq=pq):
                    for dc in range(NDC):
                        i = t.matmul(
                            psA[:, pq:pq + 512],
                            AR[:, A_QT + dc * 1024 + dcp * 128: A_QT + dc * 1024 + dcp * 128 + 128],
                            AR[:, A_R4 + dc * 2048 + qc * 512: A_R4 + dc * 2048 + qc * 512 + 512],
                            start=(dc == 0), stop=(dc == NDC - 1))
                    return i
                sb.emit("pe", mmo, reads=wout_res + at_res, writes=psq(PSA, pq, 512))
                sb.emit("dve", lambda v, dcp=dcp, qc=qc, pq=pq: v.scalar_tensor_tensor(
                    AR[:, A_KT + dcp * 2048 + qc * 512: A_KT + dcp * 2048 + qc * 512 + 512],
                    psA[:, pq:pq + 512], outb_sb[:, dcp:dcp + 1],
                    XT[:, dcp * S + qc * 512: dcp * S + qc * 512 + 512],
                    op0=AluOpType.add, op1=AluOpType.add),
                    reads=psq(PSA, pq, 512) + [("outb",), xtr(dcp, qc)],
                    writes=arb(A_KT + dcp * 2048 + qc * 512, 512))
        for qc in range(4):
            for dc in range(NDC):
                sqc = ST0 + 512 * (dc % 4)
                sb.emit("act", lambda a, dc=dc, qc=qc, sqc=sqc: a.activation(
                    AR[:, sqc:sqc + 512],
                    AR[:, A_KT + dc * 2048 + qc * 512: A_KT + dc * 2048 + qc * 512 + 512],
                    AF.Square, bias=0.0, scale=1.0),
                    reads=arb(A_KT + dc * 2048 + qc * 512, 512), writes=arb(sqc, 512))
                sb.emit("pe", lambda t, dc=dc, qc=qc: t.matmul(
                    psA[0:1, qc * 512:qc * 512 + 512], ones_col16,
                    AR[:, A_KT + dc * 2048 + qc * 512: A_KT + dc * 2048 + qc * 512 + 512],
                    start=(dc == 0), stop=(dc == NDC - 1)),
                    reads=arb(A_KT + dc * 2048 + qc * 512, 512) + [("oc16",)],
                    writes=psq(PSA, qc * 512, 512))
                sb.emit("pe", lambda t, dc=dc, qc=qc, sqc=sqc: t.matmul(
                    psB[0:1, qc * 512:qc * 512 + 512], ones_col16,
                    AR[:, sqc:sqc + 512],
                    start=(dc == 0), stop=(dc == NDC - 1)),
                    reads=arb(sqc, 512) + [("oc16",)],
                    writes=psq(PSB, qc * 512, 512))
        for qc in range(4):
            sb.emit("dve", lambda v, qc=qc: v.tensor_scalar(
                FA[0:1, 0:512], psA[0:1, qc * 512:qc * 512 + 512], 1.0 / DM, None,
                op0=AluOpType.mult),
                reads=psq(PSA, qc * 512, 512), writes=[("mrow",)])
            sb.emit("dve", lambda v, qc=qc: v.tensor_scalar(
                FA[0:1, 512:1024], psB[0:1, qc * 512:qc * 512 + 512], 1.0 / DM, None,
                op0=AluOpType.mult),
                reads=psq(PSB, qc * 512, 512), writes=[("exrow",)])
            sb.emit("dve", lambda v: v.tensor_tensor(
                FA[0:1, 1024:1536], FA[0:1, 0:512], FA[0:1, 0:512], op=AluOpType.mult),
                reads=[("mrow",)], writes=[("m2row",)])
            sb.emit("dve", lambda v: v.tensor_tensor(
                FA[0:1, 512:1024], FA[0:1, 512:1024], FA[0:1, 1024:1536],
                op=AluOpType.subtract),
                reads=[("exrow",), ("m2row",)], writes=[("varrow",)])
            sb.emit("act", lambda a: a.activation(
                FA[0:1, 1536:2048], FA[0:1, 512:1024], AF.Ln, bias=bvec[0:1, 2:3],
                scale=1.0),
                reads=[("varrow",), ("bv", 2)], writes=[("lnrow",)])
            sb.emit("act", lambda a: a.activation(
                FA[0:1, 1024:1536], FA[0:1, 1536:2048], AF.Exp, bias=0.0, scale=-0.5),
                reads=[("lnrow",)], writes=[("arow",)])
            sb.emit("dve", lambda v: v.scalar_tensor_tensor(
                FA[0:1, 0:512], FA[0:1, 0:512], -1.0, FA[0:1, 1024:1536],
                op0=AluOpType.mult, op1=AluOpType.mult),
                reads=[("mrow",), ("arow",)], writes=[("brow",)])
            sb.emit("pe", lambda t, qc=qc: t.matmul(
                psA[:, qc * 512:qc * 512 + 512], ones_row32, FA[0:1, 1024:1536],
                start=True, stop=True),
                reads=[("arow",), ("or32",)], writes=psq(PSA, qc * 512, 512))
            sb.emit("pe", lambda t, qc=qc: t.matmul(
                psB[:, qc * 512:qc * 512 + 512], ones_row32, FA[0:1, 0:512],
                start=True, stop=True),
                reads=[("brow",), ("or32",)], writes=psq(PSB, qc * 512, 512))
            sb.emit("dve", lambda v, qc=qc: v.tensor_copy(
                FA[:, 1536:2048], psA[:, qc * 512:qc * 512 + 512]),
                reads=psq(PSA, qc * 512, 512), writes=[("abc",)])
            sb.emit("dve", lambda v, qc=qc: v.tensor_copy(
                FA[:, 2048:2560], psB[:, qc * 512:qc * 512 + 512]),
                reads=psq(PSB, qc * 512, 512), writes=[("bbc",)])
            for dc in range(NDC):
                sb.emit("dve", lambda v, dc=dc, qc=qc: v.tensor_tensor(
                    AR[:, ST1:ST1 + 512],
                    AR[:, A_KT + dc * 2048 + qc * 512: A_KT + dc * 2048 + qc * 512 + 512],
                    FA[:, 1536:2048], op=AluOpType.mult),
                    reads=arb(A_KT + dc * 2048 + qc * 512, 512) + [("abc",)],
                    writes=arb(ST1, 512))
                sb.emit("dve", lambda v, dc=dc, qc=qc: v.tensor_tensor(
                    XT[:, dc * S + qc * 512: dc * S + qc * 512 + 512],
                    AR[:, ST1:ST1 + 512], FA[:, 2048:2560], op=AluOpType.add),
                    reads=arb(ST1, 512) + [("bbc",)],
                    writes=[xtr(dc, qc)] + [("xtf", dc, qc * 4 + j) for j in range(4)])

    def router_phase(s):
        for tc in range(NTC):
            def mmr(t, tc=tc):
                for dc in range(NDC):
                    i = t.matmul(psA[:, 1024:1032],
                                 XT[:, dc * S + tc * 128: dc * S + tc * 128 + 128],
                                 AR[:, RTR + dc * 8: RTR + dc * 8 + 8],
                                 start=(dc == 0), stop=(dc == NDC - 1))
                return i
            sb.emit("pe", mmr,
                    reads=[("xtf", dc, tc) for dc in range(NDC)] + arb(RTR, 64),
                    writes=psq(PSA, 1024, 512))
            sb.emit("dve", lambda v: v.reduce_max(rtc[:, 0:1], psA[:, 1024:1032], axis=AX.X),
                    reads=psq(PSA, 1024, 512), writes=[("rm",)])
            sb.emit("dve", lambda v: v.tensor_scalar(
                rtc[:, 1:2], rtc[:, 0:1], -1.0, None, op0=AluOpType.mult),
                reads=[("rm",)], writes=[("rmn",)])
            sb.emit("act", lambda a: a.activation(
                rt8, psA[:, 1024:1032], AF.Exp, bias=rtc[:, 1:2], scale=1.0),
                reads=psq(PSA, 1024, 512) + [("rmn",)], writes=[("re",)])
            sb.emit("dve", lambda v: v.reduce_sum(rtc[:, 2:3], rt8, axis=AX.X),
                    reads=[("re",)], writes=[("rs",)])
            sb.emit("dve", lambda v: v.reciprocal(rtc[:, 3:4], rtc[:, 2:3]),
                    reads=[("rs",)], writes=[("rsi",)])
            sb.emit("dve", lambda v: v.tensor_scalar(
                rt1, rt8, rtc[:, 3:4], None, op0=AluOpType.mult),
                reads=[("re",), ("rsi",)], writes=[("rp",)])
            sb.emit("dve", lambda v: v.reduce_max(rtc[:, 0:1], rt1, axis=AX.X),
                    reads=[("rp",)], writes=[("ra1",)])
            sb.emit("dve", lambda v: v.tensor_scalar(
                rt2, rt1, rtc[:, 0:1], None, op0=AluOpType.is_ge),
                reads=[("rp",), ("ra1",)], writes=[("rmask1",)])
            sb.emit("dve", lambda v: v.scalar_tensor_tensor(
                rt8, rt2, -1e9, rt1, op0=AluOpType.mult, op1=AluOpType.add),
                reads=[("rmask1",), ("rp",)], writes=[("rq",)])
            sb.emit("dve", lambda v: v.reduce_max(rtc[:, 1:2], rt8, axis=AX.X),
                    reads=[("rq",)], writes=[("rb",)])
            sb.emit("dve", lambda v: v.tensor_tensor(
                rtc[:, 2:3], rtc[:, 0:1], rtc[:, 1:2], op=AluOpType.add),
                reads=[("ra1",), ("rb",)], writes=[("rs2",)])
            sb.emit("act", lambda a: a.activation(
                rt8, rt1, AF.Exp, bias=rtc[:, 2:3], scale=-2.0),
                reads=[("rp",), ("rs2",)], writes=[("rexp",)])
            sb.emit("dve", lambda v: v.tensor_scalar(
                rt2, rt8, 1.0, None, op0=AluOpType.add),
                reads=[("rexp",)], writes=[("rd",)])
            sb.emit("dve", lambda v: v.reciprocal(rt8, rt2),
                    reads=[("rd",)], writes=[("rsg",)])
            sb.emit("dve", lambda v: v.tensor_scalar(
                rt2, rt1, rtc[:, 1:2], None, op0=AluOpType.is_ge),
                reads=[("rp",), ("rb",)], writes=[("rmask2",)])
            sb.emit("dve", lambda v, tc=tc: v.tensor_tensor(
                wfull[:, tc * 8:(tc + 1) * 8], rt8, rt2, op=AluOpType.mult),
                reads=[("rsg",), ("rmask2",)], writes=[("wf", tc)])

    def ffn_phase(s):
        for tc in range(NTC):
            sb.emit("dve", lambda v, tc=tc: v.memset(
                AR[:, A_R4 + tc * 1024: A_R4 + tc * 1024 + 1024], 0.0),
                writes=arb(A_R4 + tc * 1024, 1024))
        for e in range(E):
            slot = [0]

            def stream_w(dram_row0, dst_col):
                for hlf in range(2):
                    stg = (slot[0] % 3) * 1024
                    slot[0] += 1
                    sb.emit("sync", lambda q, hlf=hlf, stg=stg: q.dma_start(
                        FA[:, stg:stg + 1024],
                        wrow_src[0][dram_row0:dram_row0 + 128,
                                    hlf * 1024:hlf * 1024 + 1024]),
                        writes=fab(stg, 1024), dma=True)
                    sb.emit("pool", lambda g, hlf=hlf, stg=stg, dst_col=dst_col: g.tensor_copy(
                        AR[:, dst_col + hlf * 1024: dst_col + hlf * 1024 + 1024],
                        FA[:, stg:stg + 1024]),
                        reads=fab(stg, 1024),
                        writes=arb(dst_col + hlf * 1024, 1024))

            wrow_src = [wg_d]
            for dc in range(NDC):
                stream_w(e * DM + dc * 128, A_QT + dc * 2048)
            wrow_src = [wu_d]
            for dc in range(NDC):
                stream_w(e * DM + dc * 128, A_KT + dc * 2048)
            wdr = wd_d.rearrange("(r p) n -> p r n", p=128)
            for fc2 in range(8):
                for hlf in range(2):
                    stg = (slot[0] % 3) * 1024
                    slot[0] += 1
                    rbase = (e * FF + fc2 * 256) // 128 + hlf
                    sb.emit("sync", lambda q, rbase=rbase, stg=stg: q.dma_start(
                        FA[:, stg:stg + 1024],
                        wdr[:, rbase:rbase + 1, :]),
                        writes=fab(stg, 1024), dma=True)
                    sb.emit("pool", lambda g, fc2=fc2, hlf=hlf, stg=stg: g.tensor_copy(
                        AR[:, A_VT + fc2 * 2048 + hlf * 1024: A_VT + fc2 * 2048 + hlf * 1024 + 1024],
                        FA[:, stg:stg + 1024]),
                        reads=fab(stg, 1024),
                        writes=arb(A_VT + fc2 * 2048 + hlf * 1024, 1024))
            wg_all = arb(A_QT, 16384)
            wu_all = arb(A_KT, 16384)
            wd_all = arb(A_VT, 16384)
            for qc in range(4):
                xt_res = [xtr(dc, qc) for dc in range(NDC)]
                for fc in range(NFC):
                    pg = (fc % 2) * 512

                    def mmgate(t, qc=qc, fc=fc, pg=pg):
                        for dc in range(NDC):
                            i = t.matmul(
                                psA[:, pg:pg + 512],
                                AR[:, A_QT + dc * 2048 + fc * 128: A_QT + dc * 2048 + fc * 128 + 128],
                                XT[:, dc * S + qc * 512: dc * S + qc * 512 + 512],
                                start=(dc == 0), stop=(dc == NDC - 1))
                        return i
                    sb.emit("pe", mmgate, reads=xt_res + wg_all, writes=psq(PSA, pg, 512))
                    scol = SILU + (fc % 2) * 512
                    sb.emit("act", lambda a, pg=pg, scol=scol: a.activation(
                        AR[:, scol:scol + 512], psA[:, pg:pg + 512], AF.Silu,
                        bias=0.0, scale=1.0),
                        reads=psq(PSA, pg, 512), writes=[("sl", scol)])

                    def mmup(t, qc=qc, fc=fc, pg=pg):
                        for dc in range(NDC):
                            i = t.matmul(
                                psA[:, 1024 + pg:1024 + pg + 512],
                                AR[:, A_KT + dc * 2048 + fc * 128: A_KT + dc * 2048 + fc * 128 + 128],
                                XT[:, dc * S + qc * 512: dc * S + qc * 512 + 512],
                                start=(dc == 0), stop=(dc == NDC - 1))
                        return i
                    sb.emit("pe", mmup, reads=xt_res + wu_all,
                            writes=psq(PSA, 1024 + pg, 512))
                    fuse_col = FUSE + fc * 512
                    sb.emit("dve", lambda v, pg=pg, scol=scol, fuse_col=fuse_col: v.tensor_tensor(
                        AR[:, fuse_col:fuse_col + 512], AR[:, scol:scol + 512],
                        psA[:, 1024 + pg:1024 + pg + 512], op=AluOpType.mult),
                        reads=[("sl", scol)] + psq(PSA, 1024 + pg, 512),
                        writes=arb(fuse_col, 512))
                fuse_all = arb(FUSE, 8192)
                for t2 in range(4):
                    tc = qc * 4 + t2
                    for dh in range(2):
                        pg = 1024 + dh * 512

                        def mmdown(t, t2=t2, dh=dh, pg=pg):
                            for fc in range(NFC):
                                i = t.matmul(
                                    psB[:, pg:pg + 512],
                                    AR[:, FUSE + fc * 512 + t2 * 128: FUSE + fc * 512 + t2 * 128 + 128],
                                    AR[:, A_VT + fc * 1024 + dh * 512: A_VT + fc * 1024 + dh * 512 + 512],
                                    start=(fc == 0), stop=(fc == NFC - 1))
                            return i
                        sb.emit("pe", mmdown, reads=fuse_all + wd_all,
                                writes=psq(PSB, pg, 512))
                        ycol = A_R4 + tc * 1024 + dh * 512
                        sb.emit("dve", lambda v, tc=tc, pg=pg, e=e, ycol=ycol: v.scalar_tensor_tensor(
                            AR[:, ycol:ycol + 512], psB[:, pg:pg + 512],
                            wfull[:, tc * 8 + e: tc * 8 + e + 1],
                            AR[:, ycol:ycol + 512],
                            op0=AluOpType.mult, op1=AluOpType.add),
                            reads=psq(PSB, pg, 512) + [("wf", tc)] + arb(ycol, 512),
                            writes=arb(ycol, 512))

    def ln2_store(s):
        for tc in range(NTC):
            for dc in range(NDC):
                pcol = (dc % 2) * 128
                sb.emit("dve", lambda v, dc=dc, tc=tc: v.tensor_copy(
                    FA[:, 0:128], XT[:, dc * S + tc * 128: dc * S + tc * 128 + 128]),
                    reads=[("xtf", dc, tc)], writes=fab(0, 128))
                sb.emit("pe", lambda t, pcol=pcol: t.transpose(
                    psA[:, pcol:pcol + 128], FA[:, 0:128], id32),
                    reads=fab(0, 128) + [("id32",)], writes=psq(PSA, pcol, 128))
                sb.emit("dve", lambda v, dc=dc, tc=tc, pcol=pcol: v.tensor_tensor(
                    FA[:, 1024 + dc * 128:1024 + dc * 128 + 128],
                    psA[:, pcol:pcol + 128],
                    AR[:, A_R4 + tc * 1024 + dc * 128: A_R4 + tc * 1024 + dc * 128 + 128],
                    op=AluOpType.add),
                    reads=psq(PSA, pcol, 128) + arb(A_R4 + tc * 1024 + dc * 128, 128),
                    writes=fab(1024 + dc * 128, 128))
            sb.emit("dve", lambda v: v.reduce_sum(stat[:, 0:1], FA[:, 1024:2048], axis=AX.X),
                    reads=fab(1024, 1024), writes=[("l2s1",)])
            sb.emit("act", lambda a: a.activation(
                AR[:, ST0:ST0 + 1024], FA[:, 1024:2048], AF.Square, bias=0.0, scale=1.0),
                reads=fab(1024, 1024), writes=arb(ST0, 1024))
            sb.emit("dve", lambda v: v.reduce_sum(
                stat[:, 1:2], AR[:, ST0:ST0 + 1024], axis=AX.X),
                reads=arb(ST0, 1024), writes=[("l2s2",)])
            sb.emit("dve", lambda v: v.tensor_scalar(
                stat[:, 2:3], stat[:, 0:1], 1.0 / DM, None, op0=AluOpType.mult),
                reads=[("l2s1",)], writes=[("l2m",)])
            sb.emit("dve", lambda v: v.tensor_tensor(
                stat[:, 3:4], stat[:, 2:3], stat[:, 2:3], op=AluOpType.mult),
                reads=[("l2m",)], writes=[("l2m2",)])
            sb.emit("dve", lambda v: v.scalar_tensor_tensor(
                stat[:, 4:5], stat[:, 1:2], 1.0 / DM, stat[:, 3:4],
                op0=AluOpType.mult, op1=AluOpType.subtract),
                reads=[("l2s2",), ("l2m2",)], writes=[("l2var",)])
            sb.emit("act", lambda a: a.activation(
                stat[:, 5:6], stat[:, 4:5], AF.Ln, bias=bvec[:, 2:3], scale=1.0),
                reads=[("l2var",), ("bv", 2)], writes=[("l2ln",)])
            sb.emit("act", lambda a: a.activation(
                stat[:, 6:7], stat[:, 5:6], AF.Exp, bias=0.0, scale=-0.5),
                reads=[("l2ln",)], writes=[("l2rs",)])
            sb.emit("dve", lambda v: v.tensor_scalar(
                stat[:, 7:8], stat[:, 2:3], -1.0, None, op0=AluOpType.mult),
                reads=[("l2m",)], writes=[("l2mn",)])
            sb.emit("dve", lambda v: v.tensor_scalar(
                FA[:, 2048:3072], FA[:, 1024:2048], stat[:, 7:8], stat[:, 6:7],
                op0=AluOpType.add, op1=AluOpType.mult),
                reads=fab(1024, 1024) + [("l2mn",), ("l2rs",)],
                writes=fab(2048, 1024))
            sb.emit("sync", lambda q, tc=tc, s=s: q.dma_start(
                o2[s * S + tc * 128: s * S + (tc + 1) * 128, :], FA[:, 2048:3072]),
                reads=fab(2048, 1024), writes=[("store", s, tc)], dma=True)

    def dump_region(dst_ap, region, width_each, n, res_of):
        for i in range(n):
            sb.emit("dve", lambda v, i=i: v.tensor_copy(
                FA[:, 0:width_each],
                AR[:, region + i * width_each: region + (i + 1) * width_each]),
                reads=res_of(i), writes=fab(0, width_each))
            sb.emit("sync", lambda q, i=i: q.dma_start(
                dst_ap[i], FA[:, 0:width_each]),
                reads=fab(0, width_each), writes=[("dmp",)], dma=True)

    nseq = 1 if debug == 8 else B_CORE
    for s in range(nseq):
        load_xt(s)
        if bmask & 1:
            sb.barrier()
        qkv_phase(s)
        if bmask & 2:
            sb.barrier()
        if debug and debug != 8 and s == 0:
            dump_region(dbg["q"], A_QT, 2048, NDC, lambda i: arb(A_QT + i * 2048, 2048))
            dump_region(dbg["k"], A_KT, 2048, NDC, lambda i: arb(A_KT + i * 2048, 2048))
            dump_region(dbg["v"], A_VT, 1024, NTC, lambda i: arb(A_VT + i * 1024, 1024))
            if debug == 2:
                break
        attn_phase(s)
        if bmask & 4:
            sb.barrier()
        if debug and debug != 8 and s == 0:
            dump_region(dbg["attnT"], A_R4, 2048, NDC, lambda i: arb(A_R4 + i * 2048, 2048))
            if debug in (3, 31):
                break
        outproj_ln1(s)
        if bmask & 8:
            sb.barrier()
        if debug and debug != 8 and s == 0:
            for dc in range(NDC):
                sb.emit("dve", lambda v, dc=dc: v.tensor_copy(
                    FA[:, 0:2048], XT[:, dc * S:(dc + 1) * S]),
                    reads=[xtr(dc, q_) for q_ in range(4)], writes=fab(0, 2048))
                sb.emit("sync", lambda q, dc=dc: q.dma_start(
                    dbg["x1"][dc], FA[:, 0:2048]),
                    reads=fab(0, 2048), writes=[("dmp",)], dma=True)
            if debug == 4:
                break
        router_phase(s)
        if bmask & 16:
            sb.barrier()
        if debug and debug != 8 and s == 0:
            for tc in range(NTC):
                sb.emit("sync", lambda q, tc=tc: q.dma_start(
                    dbg["wfull"][tc], wfull[:, tc * 8:(tc + 1) * 8]),
                    reads=[("wf", tc)], writes=[("dmp",)], dma=True)
            if debug == 5:
                break
        ffn_phase(s)
        if bmask & 32:
            sb.barrier()
        if debug and debug != 8 and s == 0:
            dump_region(dbg["y"], A_R4, 1024, NTC, lambda i: arb(A_R4 + i * 1024, 1024))
            if debug == 6:
                break
        ln2_store(s)
        if bmask & 64:
            sb.barrier()
        if debug and debug != 8:
            break
    nc.sync.wait_ge(sb.sem["sync"], sb.cnt["sync"])
    return nc


def kernel(L0_attn_x, L0_attn_qkv_w, L0_attn_qkv_b, L0_attn_out_w, L0_attn_out_b,
           L0_ffn_router_kernel_DE, L0_ffn_kernel_gating_EDF, L0_ffn_kernel_up_EDF,
           L0_ffn_kernel_down_EFD, _debug=0):
    x = np.ascontiguousarray(np.asarray(L0_attn_x, dtype=np.float32))
    common = {
        "qkv_w": np.ascontiguousarray(np.asarray(L0_attn_qkv_w, np.float32)),
        "qkv_b": np.ascontiguousarray(np.asarray(L0_attn_qkv_b, np.float32)),
        "out_w": np.ascontiguousarray(np.asarray(L0_attn_out_w, np.float32)),
        "out_b": np.ascontiguousarray(np.asarray(L0_attn_out_b, np.float32)),
        "router": np.ascontiguousarray(np.asarray(L0_ffn_router_kernel_DE, np.float32)),
        "w_gate": np.ascontiguousarray(
            np.asarray(L0_ffn_kernel_gating_EDF, np.float32).reshape(E * DM, FF)),
        "w_up": np.ascontiguousarray(
            np.asarray(L0_ffn_kernel_up_EDF, np.float32).reshape(E * DM, FF)),
        "w_down": np.ascontiguousarray(
            np.asarray(L0_ffn_kernel_down_EFD, np.float32).reshape(E * FF, DM)),
    }
    nc = build_program(debug=_debug, bmask=127)
    in_maps = []
    for c in range(NCORES):
        m = dict(common)
        m["x"] = x[c * B_CORE:(c + 1) * B_CORE]
        in_maps.append(m)
    res = run_bass_kernel_spmd(nc, in_maps, core_ids=list(range(NCORES)))
    out = np.concatenate([np.asarray(r["out"], np.float32) for r in res.results], axis=0)
    if _debug:
        return out, res.results
    return out


# version 04
# speedup vs baseline: 1.2232x; 1.2232x over previous
import numpy as np
import concourse.bass as bass
import concourse.mybir as mybir
from concourse.bass_utils import run_bass_kernel_spmd
from concourse.alu_op_type import AluOpType

F32 = mybir.dt.float32
BF16 = mybir.dt.bfloat16
I32 = mybir.dt.int32
AF = mybir.ActivationFunctionType
AX = mybir.AxisListType

B_FULL, S, DM = 16, 2048, 1024
NCORES = 8
B_CORE = B_FULL // NCORES
H, HD = 16, 64
E, FF = 8, 2048
NTC = S // 128
NDC = DM // 128
NFC = FF // 128

G1, LA1, G2, PA, PB, PC, PBIAS = (9.33208063, -6.83677266, 0.31989422,
                                  -16.53699443, 30.24374182, 170.18486467,
                                  -249.23997529)
SSHIFT = 40.0

A_QT = 0
A_KT = 16384
A_VT = 32768
A_R4 = 49152
A_SP = 73728
AR_COLS = 81920

ST0 = A_SP
ST1 = A_SP + 2048
BIAS512 = A_SP + 4096
QHAT = A_SP + 4608
SILU = A_SP + 5120
COLV = A_SP + 6144
RTR = A_SP + 6400
FUSE = A_R4 + 16384

FA_COLS = 3072


class SB:

    ENGS = ("pe", "act", "dve", "pool", "sync")

    def __init__(self, nc):
        self.nc = nc
        self.h = {"pe": nc.tensor, "act": nc.scalar, "dve": nc.vector,
                  "pool": nc.gpsimd, "sync": nc.sync}
        self.sem = {e: nc.alloc_semaphore(name=f"sb_{e}") for e in self.ENGS}
        self.cnt = {e: 0 for e in self.ENGS}
        self.sat = {c: {p: 0 for p in self.ENGS} for c in self.ENGS}
        self.res = {}

    def barrier(self):
        for c in self.ENGS:
            h = self.h[c]
            for p in self.ENGS:
                if p == c or self.cnt[p] == 0:
                    continue
                if self.cnt[p] > self.sat[c][p]:
                    h.wait_ge(self.sem[p], self.cnt[p])
                    self.sat[c][p] = self.cnt[p]

    def emit(self, engine, fn, reads=(), writes=(), dma=False):
        deps = {}

        def dep(eng, cntv):
            if cntv > deps.get(eng, 0):
                deps[eng] = cntv

        for r in reads:
            e = self.res.get(r)
            if e and e["w"]:
                dep(*e["w"])
        for w in writes:
            e = self.res.get(w)
            if e:
                if e["w"]:
                    dep(*e["w"])
                for pr in e["r"]:
                    dep(*pr)
        hh = self.h[engine]
        for peng in self.ENGS:
            pcnt = deps.get(peng, 0)
            if pcnt > self.sat[engine][peng]:
                hh.wait_ge(self.sem[peng], pcnt)
                self.sat[engine][peng] = pcnt
        inst = fn(hh)
        inst.then_inc(self.sem[engine], 16 if dma else 1)
        self.cnt[engine] += 16 if dma else 1
        my = (engine, self.cnt[engine])
        for r in reads:
            self.res.setdefault(r, {"w": None, "r": []})["r"].append(my)
        for w in writes:
            self.res[w] = {"w": my, "r": []}
        return inst


def build_program(debug=0, bmask=0, pstop=0):
    nc = bass.Bass("TRN2", target_bir_lowering=False)

    x_d = nc.dram_tensor("x", [B_CORE, S, DM], F32, kind="ExternalInput").ap()
    qkvw_d = nc.dram_tensor("qkv_w", [3 * DM, DM], F32, kind="ExternalInput").ap()
    qkvb_d = nc.dram_tensor("qkv_b", [3 * DM], F32, kind="ExternalInput").ap()
    outw_d = nc.dram_tensor("out_w", [DM, DM], F32, kind="ExternalInput").ap()
    outb_d = nc.dram_tensor("out_b", [DM], F32, kind="ExternalInput").ap()
    rtr_d = nc.dram_tensor("router", [DM, E], F32, kind="ExternalInput").ap()
    wg_d = nc.dram_tensor("w_gate", [E * DM, FF], F32, kind="ExternalInput").ap()
    wu_d = nc.dram_tensor("w_up", [E * DM, FF], F32, kind="ExternalInput").ap()
    wd_d = nc.dram_tensor("w_down", [E * FF, DM], F32, kind="ExternalInput").ap()
    out_d = nc.dram_tensor("out", [B_CORE, S, DM], F32, kind="ExternalOutput").ap()
    x2 = x_d.rearrange("b s d -> (b s) d")
    o2 = out_d.rearrange("b s d -> (b s) d")

    dbg = {}
    if debug:
        for nm, shp in [("q", [NDC, 128, S]), ("k", [NDC, 128, S]),
                        ("v", [NTC, 128, DM]), ("attnT", [NDC, 128, S]),
                        ("x1", [NDC, 128, S]), ("wfull", [NTC, 128, E]),
                        ("y", [NTC, 128, DM]),
                        ("st", [NTC, 128, S]), ("cs", [NTC, 128, 2]),
                        ("u", [1, 128, S]), ("svout", [1, 128, S]),
                        ("dena", [1, 128, S]), ("colv", [NTC, 128, 65]), ("colx", [NTC, 128, 4])]:
            dbg[nm] = nc.dram_tensor("dbg_" + nm, shp, F32, kind="ExternalOutput").ap()

    AR = nc.alloc_sbuf_tensor("AR", [128, AR_COLS], BF16).ap()
    FA = nc.alloc_sbuf_tensor("FA", [128, FA_COLS], F32).ap()
    XT = nc.alloc_sbuf_tensor("XT", [128, NDC * S], BF16).ap()
    id32 = nc.alloc_sbuf_tensor("id32", [128, 128], F32).ap()
    id16 = nc.alloc_sbuf_tensor("id16", [128, 128], BF16).ap()
    io_a = nc.alloc_sbuf_tensor("io_a", [128, 128], I32).ap()
    io_b = nc.alloc_sbuf_tensor("io_b", [128, 128], I32).ap()
    ones_col16 = nc.alloc_sbuf_tensor("ones_col16", [128, 1], BF16).ap()
    ones_row32 = nc.alloc_sbuf_tensor("ones_row32", [1, 128], F32).ap()
    ones64 = nc.alloc_sbuf_tensor("ones64", [65, 64], F32).ap()
    bvec = nc.alloc_sbuf_tensor("bvec", [128, 8], F32).ap()
    outb_sb = nc.alloc_sbuf_tensor("outb_sb", [128, NDC], F32).ap()
    wfull = nc.alloc_sbuf_tensor("wfull", [128, NTC * E], F32).ap()
    rt8 = nc.alloc_sbuf_tensor("rt8", [128, 8], F32).ap()
    rt1 = nc.alloc_sbuf_tensor("rt1", [128, 8], F32).ap()
    rt2 = nc.alloc_sbuf_tensor("rt2", [128, 8], F32).ap()
    rtc = nc.alloc_sbuf_tensor("rtc", [128, 6], F32).ap()
    ss8 = nc.alloc_sbuf_tensor("ss8", [128, 16], F32).ap()
    cs2 = nc.alloc_sbuf_tensor("cs2", [128, 4], F32).ap()
    csx = nc.alloc_sbuf_tensor("csx", [128, 4], F32).ap()
    stat = nc.alloc_sbuf_tensor("stat", [128, 16], F32).ap()

    psA = nc.alloc_psum_tensor("psA", [128, 2048], F32).ap()
    psB = nc.alloc_psum_tensor("psB", [128, 2048], F32).ap()

    sb = SB(nc)

    def arb(col, width):
        return [("ar", b) for b in range(col // 2048, (col + width + 2047) // 2048)]

    def fab(col, width):
        return [("fa", b) for b in range(col // 512, (col + width + 511) // 512)]

    def psq(t, col, width):
        return [(t, b) for b in range(col // 512, (col + width + 511) // 512)]

    PSA, PSB = "psA", "psB"
    xtr = lambda dc, blk: ("xt", dc, blk)

    sb.emit("pool", lambda g: g.iota(io_a, [[1, 128]], base=0, channel_multiplier=0),
            writes=[("io", 0)])
    sb.emit("pool", lambda g: g.iota(io_b, [[0, 128]], base=0, channel_multiplier=1),
            writes=[("io", 1)])
    sb.emit("dve", lambda v: v.tensor_tensor(id32, io_b, io_a, op=AluOpType.is_equal),
            reads=[("io", 0), ("io", 1)], writes=[("id32",)])
    sb.emit("dve", lambda v: v.tensor_copy(id16, id32),
            reads=[("id32",)], writes=[("id16",)])
    sb.emit("dve", lambda v: v.memset(ones_col16, 1.0), writes=[("oc16",)])
    sb.emit("dve", lambda v: v.memset(ones_row32, 1.0), writes=[("or32",)])
    sb.emit("dve", lambda v: v.memset(ones64, 1.0), writes=[("o64",)])
    for i, val in [(0, PBIAS + SSHIFT), (1, LA1), (2, 1e-5), (3, 1e-20), (4, PB)]:
        sb.emit("dve", lambda v, i=i, val=val: v.memset(bvec[:, i:i + 1], val),
                writes=[("bv", i)])
    sb.emit("sync", lambda q: q.dma_start(
        outb_sb, outb_d.rearrange("(c p) -> p c", p=128),
        allow_slow_non_contiguous=True), writes=[("outb",)], dma=True)
    for dc in range(NDC):
        sb.emit("sync", lambda q, dc=dc: q.dma_start(
            FA[:, dc * 8:dc * 8 + 8], rtr_d[dc * 128:(dc + 1) * 128, :]),
            writes=fab(dc * 8, 8), dma=True)
    sb.emit("pool", lambda g: g.tensor_copy(AR[:, RTR:RTR + 64], FA[:, 0:64]),
            reads=fab(0, 64), writes=arb(RTR, 64))

    def load_xt(s):
        for tc in range(NTC):
            stg = (tc % 2) * 1024
            sb.emit("sync", lambda q, tc=tc, stg=stg: q.dma_start(
                FA[:, stg:stg + 1024], x2[s * S + tc * 128: s * S + (tc + 1) * 128, :]),
                writes=fab(stg, 1024), dma=True)
            for dc in range(NDC):
                pcol = (dc % 2) * 128
                sb.emit("pe", lambda t, dc=dc, stg=stg, pcol=pcol: t.transpose(
                    psB[:, pcol:pcol + 128], FA[:, stg + dc * 128:stg + dc * 128 + 128], id32),
                    reads=fab(stg + dc * 128, 128) + [("id32",)],
                    writes=psq(PSB, pcol, 128))
                sb.emit("dve", lambda v, dc=dc, tc=tc, pcol=pcol: v.tensor_copy(
                    XT[:, dc * S + tc * 128: dc * S + tc * 128 + 128], psB[:, pcol:pcol + 128]),
                    reads=psq(PSB, pcol, 128), writes=[xtr(dc, tc // 4), ("xtf", dc, tc)])

    def load_transpose_w(dram, rows, wcol_base, chunk_cols):
        for rc in range(rows // 128):
            stg = 2048
            sb.emit("sync", lambda q, rc=rc: q.dma_start(
                FA[:, stg:stg + 1024], dram[rc * 128:(rc + 1) * 128, :]),
                writes=fab(stg, 1024), dma=True)
            for dc in range(NDC):
                pcol = 256 + (dc % 2) * 128
                sb.emit("pe", lambda t, dc=dc, pcol=pcol: t.transpose(
                    psB[:, pcol:pcol + 128], FA[:, stg + dc * 128:stg + dc * 128 + 128], id32),
                    reads=fab(stg + dc * 128, 128) + [("id32",)],
                    writes=psq(PSB, pcol, 128))
                dst = wcol_base + dc * chunk_cols + rc * 128
                sb.emit("dve", lambda v, dst=dst, pcol=pcol: v.tensor_copy(
                    AR[:, dst:dst + 128], psB[:, pcol:pcol + 128]),
                    reads=psq(PSB, pcol, 128), writes=arb(dst, 128))

    def qkv_phase(s):
        load_transpose_w(qkvw_d, 3 * DM, A_R4, 3072)
        wqkv_res = arb(A_R4, 24576)
        for fch in range(6):
            fbase = fch * 512
            sb.emit("sync", lambda q, fbase=fbase: q.dma_start(
                FA[0:1, 1536:2048], qkvb_d[fbase:fbase + 512].rearrange("(o n) -> o n", o=1)),
                writes=[("qbrow",)], dma=True)
            sb.emit("pe", lambda t: t.matmul(
                psB[:, 1536:2048], ones_row32, FA[0:1, 1536:2048], start=True, stop=True),
                reads=[("qbrow",), ("or32",)], writes=psq(PSB, 1536, 512))
            sb.emit("dve", lambda v: v.tensor_copy(FA[:, 1024:1536], psB[:, 1536:2048]),
                    reads=psq(PSB, 1536, 512), writes=fab(1024, 512))
            for tc in range(NTC):
                pq = (tc % 2) * 512

                def mmg(t, tc=tc, fbase=fbase, pq=pq):
                    for dc in range(NDC):
                        i = t.matmul(psA[:, pq:pq + 512],
                                     XT[:, dc * S + tc * 128: dc * S + tc * 128 + 128],
                                     AR[:, A_R4 + dc * 3072 + fbase: A_R4 + dc * 3072 + fbase + 512],
                                     start=(dc == 0), stop=(dc == NDC - 1))
                    return i
                sb.emit("pe", mmg,
                        reads=[("xtf", dc, tc) for dc in range(NDC)] + wqkv_res,
                        writes=psq(PSA, pq, 512))
                sb.emit("dve", lambda v, pq=pq: v.tensor_tensor(
                    FA[:, 0:512], psA[:, pq:pq + 512], FA[:, 1024:1536], op=AluOpType.add),
                    reads=psq(PSA, pq, 512) + fab(1024, 512), writes=fab(0, 512))
                if fch >= 4:
                    vcol = A_VT + tc * 1024 + (fch - 4) * 512
                    sb.emit("dve", lambda v, vcol=vcol: v.tensor_copy(
                        AR[:, vcol:vcol + 512], FA[:, 0:512]),
                        reads=fab(0, 512), writes=arb(vcol, 512))
                else:
                    for hh in range(8):
                        sb.emit("dve", lambda v, hh=hh: v.scalar_tensor_tensor(
                            FA[:, 512 + hh * 64:512 + hh * 64 + 64],
                            FA[:, hh * 64:hh * 64 + 64], 1.0,
                            FA[:, hh * 64:hh * 64 + 64],
                            op0=AluOpType.mult, op1=AluOpType.mult,
                            accum_out=ss8[:, hh:hh + 1]),
                            reads=fab(0, 512), writes=fab(512, 64) + [("ss", hh)])
                    sb.emit("act", lambda a: a.activation(
                        ss8[:, 8:16], ss8[:, 0:8], AF.Ln, bias=bvec[:, 3:4], scale=1.0),
                        reads=[("ss", hh) for hh in range(8)] + [("bv", 3)],
                        writes=[("ssl",)])
                    sb.emit("act", lambda a: a.activation(
                        ss8[:, 0:8], ss8[:, 8:16], AF.Exp, bias=0.0, scale=-0.5),
                        reads=[("ssl",)], writes=[("rn",)] + [("ss", hh) for hh in range(8)])
                    for hh in range(8):
                        sb.emit("dve", lambda v, hh=hh: v.tensor_scalar(
                            FA[:, 512 + hh * 64:512 + hh * 64 + 64],
                            FA[:, hh * 64:hh * 64 + 64],
                            ss8[:, hh:hh + 1], None, op0=AluOpType.mult),
                            reads=fab(0, 512) + [("rn",)] + fab(512 + hh * 64, 64),
                            writes=[("qhf", hh)])
                    base_region = A_QT if fch < 2 else A_KT
                    for pp in range(4):
                        chunk = (fch % 2) * 4 + pp
                        pcol = 1024 + (pp % 4) * 128
                        sb.emit("pe", lambda t, pp=pp, pcol=pcol: t.transpose(
                            psA[:, pcol:pcol + 128],
                            FA[:, 512 + pp * 128:512 + pp * 128 + 128], id32),
                            reads=[("qhf", 2 * pp), ("qhf", 2 * pp + 1), ("id32",)],
                            writes=psq(PSA, pcol, 128))
                        dst = base_region + chunk * 2048 + tc * 128
                        sb.emit("dve", lambda v, dst=dst, pcol=pcol: v.tensor_copy(
                            AR[:, dst:dst + 128], psA[:, pcol:pcol + 128]),
                            reads=psq(PSA, pcol, 128), writes=arb(dst, 128))

    def attn_phase(s):
        for head in range(H):
            chunk, prow = head // 2, (head % 2) * 64
            qt_res = arb(A_QT + chunk * 2048, 2048)
            kt_res = arb(A_KT + chunk * 2048, 2048)
            def make_expf(par, stc, q4):
                def go():
                    sb.emit("act", lambda a: a.activation(
                        AR[:, stc + q4 * 512:stc + q4 * 512 + 512], FA[:, par:par + 512],
                        AF.Exp, bias=bvec[:, 0:1], scale=1.0,
                        accum_out=cs2[:, q4:q4 + 1]),
                        reads=fab(par, 512) + [("bv", 0)],
                        writes=arb(stc + q4 * 512, 512) + [("cs", q4)])
                return go

            def kc_final(kc, stc, head=head, chunk=chunk, prow=prow):
                sb.emit("act", lambda a: a.activation(
                    ss8[:, 12:16], cs2[:, 0:4], AF.Copy, bias=0.0, scale=1.0,
                    accum_out=csx[:, 0:1]),
                    reads=[("cs", i) for i in range(4)], writes=[("css",), ("cssj",)])
                sb.emit("act", lambda a: a.activation(
                    csx[:, 1:2], csx[:, 0:1], AF.Ln, bias=0.0, scale=1.0),
                    reads=[("css",)], writes=[("csl",)])
                sb.emit("act", lambda a: a.activation(
                    csx[:, 2:3], csx[:, 1:2], AF.Exp, bias=0.0, scale=-0.5),
                    reads=[("csl",)], writes=[("col",)])
                cvc = COLV + (kc % 2) * 65
                sb.emit("dve", lambda v, kc=kc, head=head, cvc=cvc: v.tensor_scalar(
                    AR[:, cvc:cvc + 64],
                    AR[:, A_VT + kc * 1024 + head * 64: A_VT + kc * 1024 + head * 64 + 64],
                    csx[:, 2:3], None, op0=AluOpType.mult),
                    reads=[("col",)] + arb(A_VT + kc * 1024 + head * 64, 64),
                    writes=[("colv", kc % 2)])
                sb.emit("dve", lambda v, cvc=cvc: v.tensor_copy(
                    AR[:, cvc + 64:cvc + 65], csx[:, 2:3]),
                    reads=[("col",)], writes=[("colvd", kc % 2)])

                def mmsv(t, kc=kc, stc=stc, cvc=cvc):
                    for qc in range(4):
                        i = t.matmul(psB[0:65, qc * 512:(qc + 1) * 512],
                                     AR[:, cvc:cvc + 65],
                                     AR[:, stc + qc * 512:stc + qc * 512 + 512],
                                     start=(kc == 0), stop=(kc == NTC - 1))
                    return i
                sb.emit("pe", mmsv,
                        reads=[("colv", kc % 2), ("colvd", kc % 2)] + arb(stc, 2048),
                        writes=psq(PSB, 0, 2048))

            pend = None
            for kc in range(NTC):
                stc = ST0 if kc % 2 == 0 else ST1
                for q4 in range(4):
                    par = ((kc * 4 + q4) % 2) * 1536
                    pq = q4 * 512
                    sb.emit("pe", lambda t, kc=kc, chunk=chunk, prow=prow, pq=pq: t.matmul(
                        psA[:, pq:pq + 512],
                        AR[prow:prow + 64,
                           A_KT + chunk * 2048 + kc * 128: A_KT + chunk * 2048 + kc * 128 + 128],
                        AR[prow:prow + 64,
                           A_QT + chunk * 2048 + pq: A_QT + chunk * 2048 + pq + 512],
                        start=True, stop=True),
                        reads=qt_res + kt_res, writes=psq(PSA, pq, 512))
                    sb.emit("act", lambda a, par=par, pq=pq: a.activation(
                        FA[:, par:par + 512], psA[:, pq:pq + 512], AF.Exp,
                        bias=bvec[:, 1:2], scale=G1),
                        reads=psq(PSA, pq, 512) + [("bv", 1)], writes=fab(par, 512))
                    sb.emit("act", lambda a, par=par, pq=pq: a.activation(
                        FA[:, par + 512:par + 1024], psA[:, pq:pq + 512], AF.Exp,
                        bias=0.0, scale=G2),
                        reads=psq(PSA, pq, 512), writes=fab(par + 512, 512))
                    sb.emit("act", lambda a, par=par: a.activation(
                        FA[:, par + 1024:par + 1536], FA[:, par + 512:par + 1024],
                        AF.Square, bias=bvec[:, 4:5], scale=PA),
                        reads=fab(par + 512, 512) + [("bv", 4)],
                        writes=fab(par + 1024, 512))
                    sb.emit("pool", lambda g, par=par: g.tensor_tensor(
                        FA[:, par + 512:par + 1024], FA[:, par:par + 512],
                        FA[:, par + 1024:par + 1536], op=AluOpType.add),
                        reads=fab(par, 512) + fab(par + 1024, 512),
                        writes=fab(par + 512, 512))
                    sb.emit("dve", lambda v, par=par, pq=pq: v.scalar_tensor_tensor(
                        FA[:, par:par + 512], psA[:, pq:pq + 512], PC,
                        FA[:, par + 512:par + 1024],
                        op0=AluOpType.mult, op1=AluOpType.add),
                        reads=psq(PSA, pq, 512) + fab(par + 512, 512),
                        writes=fab(par, 512))
                    if pend is not None:
                        pend()
                    pend = make_expf(par, stc, q4)
                    if q4 == 0 and kc > 0:
                        kc_final(kc - 1, ST0 if (kc - 1) % 2 == 0 else ST1)
            pend()
            pend = None
            kc_final(NTC - 1, ST0 if (NTC - 1) % 2 == 0 else ST1)
            sb.emit("dve", lambda v: v.reciprocal(FA[64:65, 1024:3072], psB[64:65, 0:2048]),
                    reads=psq(PSB, 0, 2048) + fab(1024, 2048), writes=[("denr",)])
            if debug == 31 and head == 0:
                sb.emit("sync", lambda q: q.dma_start(dbg["dena"][0], FA[:, 1024:3072]),
                        reads=[("denr",)] + fab(1024, 2048), writes=[("dmp",)], dma=True)
            for qc in range(4):
                pcol = 1024 + (qc % 2) * 512
                fcol = (qc % 2) * 512
                sb.emit("pe", lambda t, qc=qc, pcol=pcol: t.matmul(
                    psA[0:64, pcol:pcol + 512], ones64[64:65, :],
                    FA[64:65, 1024 + qc * 512:1024 + qc * 512 + 512], start=True, stop=True),
                    reads=[("denr",), ("o64",)], writes=psq(PSA, pcol, 512))
                sb.emit("dve", lambda v, pcol=pcol, fcol=fcol: v.tensor_copy(
                    FA[0:64, fcol:fcol + 512], psA[0:64, pcol:pcol + 512]),
                    reads=psq(PSA, pcol, 512), writes=[("dbc", fcol)] + fab(fcol, 512))
                dst = A_R4 + chunk * 2048 + qc * 512
                sb.emit("dve", lambda v, qc=qc, dst=dst, prow=prow, fcol=fcol: v.tensor_tensor(
                    AR[prow:prow + 64, dst:dst + 512], psB[0:64, qc * 512:qc * 512 + 512],
                    FA[0:64, fcol:fcol + 512], op=AluOpType.mult),
                    reads=psq(PSB, qc * 512, 512) + [("dbc", fcol)] + fab(fcol, 512),
                    writes=arb(dst, 512))

    def outproj_ln1(s):
        load_transpose_w(outw_d, DM, A_QT, 1024)
        wout_res = arb(A_QT, 8192)
        at_res = arb(A_R4, 16384)
        for dcp in range(NDC):
            for qc in range(4):
                pq = (qc % 2) * 512

                def mmo(t, dcp=dcp, qc=qc, pq=pq):
                    for dc in range(NDC):
                        i = t.matmul(
                            psA[:, pq:pq + 512],
                            AR[:, A_QT + dc * 1024 + dcp * 128: A_QT + dc * 1024 + dcp * 128 + 128],
                            AR[:, A_R4 + dc * 2048 + qc * 512: A_R4 + dc * 2048 + qc * 512 + 512],
                            start=(dc == 0), stop=(dc == NDC - 1))
                    return i
                sb.emit("pe", mmo, reads=wout_res + at_res, writes=psq(PSA, pq, 512))
                sb.emit("dve", lambda v, dcp=dcp, qc=qc, pq=pq: v.scalar_tensor_tensor(
                    AR[:, A_KT + dcp * 2048 + qc * 512: A_KT + dcp * 2048 + qc * 512 + 512],
                    psA[:, pq:pq + 512], outb_sb[:, dcp:dcp + 1],
                    XT[:, dcp * S + qc * 512: dcp * S + qc * 512 + 512],
                    op0=AluOpType.add, op1=AluOpType.add),
                    reads=psq(PSA, pq, 512) + [("outb",), xtr(dcp, qc)],
                    writes=arb(A_KT + dcp * 2048 + qc * 512, 512))
        for qc in range(4):
            for dc in range(NDC):
                sqc = ST0 + 512 * (dc % 4)
                sb.emit("act", lambda a, dc=dc, qc=qc, sqc=sqc: a.activation(
                    AR[:, sqc:sqc + 512],
                    AR[:, A_KT + dc * 2048 + qc * 512: A_KT + dc * 2048 + qc * 512 + 512],
                    AF.Square, bias=0.0, scale=1.0),
                    reads=arb(A_KT + dc * 2048 + qc * 512, 512), writes=arb(sqc, 512))
                sb.emit("pe", lambda t, dc=dc, qc=qc: t.matmul(
                    psA[0:1, qc * 512:qc * 512 + 512], ones_col16,
                    AR[:, A_KT + dc * 2048 + qc * 512: A_KT + dc * 2048 + qc * 512 + 512],
                    start=(dc == 0), stop=(dc == NDC - 1)),
                    reads=arb(A_KT + dc * 2048 + qc * 512, 512) + [("oc16",)],
                    writes=psq(PSA, qc * 512, 512))
                sb.emit("pe", lambda t, dc=dc, qc=qc, sqc=sqc: t.matmul(
                    psB[0:1, qc * 512:qc * 512 + 512], ones_col16,
                    AR[:, sqc:sqc + 512],
                    start=(dc == 0), stop=(dc == NDC - 1)),
                    reads=arb(sqc, 512) + [("oc16",)],
                    writes=psq(PSB, qc * 512, 512))
        for qc in range(4):
            sb.emit("dve", lambda v, qc=qc: v.tensor_scalar(
                FA[0:1, 0:512], psA[0:1, qc * 512:qc * 512 + 512], 1.0 / DM, None,
                op0=AluOpType.mult),
                reads=psq(PSA, qc * 512, 512), writes=[("mrow",)])
            sb.emit("dve", lambda v, qc=qc: v.tensor_scalar(
                FA[0:1, 512:1024], psB[0:1, qc * 512:qc * 512 + 512], 1.0 / DM, None,
                op0=AluOpType.mult),
                reads=psq(PSB, qc * 512, 512), writes=[("exrow",)])
            sb.emit("dve", lambda v: v.tensor_tensor(
                FA[0:1, 1024:1536], FA[0:1, 0:512], FA[0:1, 0:512], op=AluOpType.mult),
                reads=[("mrow",)], writes=[("m2row",)])
            sb.emit("dve", lambda v: v.tensor_tensor(
                FA[0:1, 512:1024], FA[0:1, 512:1024], FA[0:1, 1024:1536],
                op=AluOpType.subtract),
                reads=[("exrow",), ("m2row",)], writes=[("varrow",)])
            sb.emit("act", lambda a: a.activation(
                FA[0:1, 1536:2048], FA[0:1, 512:1024], AF.Ln, bias=bvec[0:1, 2:3],
                scale=1.0),
                reads=[("varrow",), ("bv", 2)], writes=[("lnrow",)])
            sb.emit("act", lambda a: a.activation(
                FA[0:1, 1024:1536], FA[0:1, 1536:2048], AF.Exp, bias=0.0, scale=-0.5),
                reads=[("lnrow",)], writes=[("arow",)])
            sb.emit("dve", lambda v: v.scalar_tensor_tensor(
                FA[0:1, 0:512], FA[0:1, 0:512], -1.0, FA[0:1, 1024:1536],
                op0=AluOpType.mult, op1=AluOpType.mult),
                reads=[("mrow",), ("arow",)], writes=[("brow",)])
            sb.emit("pe", lambda t, qc=qc: t.matmul(
                psA[:, qc * 512:qc * 512 + 512], ones_row32, FA[0:1, 1024:1536],
                start=True, stop=True),
                reads=[("arow",), ("or32",)], writes=psq(PSA, qc * 512, 512))
            sb.emit("pe", lambda t, qc=qc: t.matmul(
                psB[:, qc * 512:qc * 512 + 512], ones_row32, FA[0:1, 0:512],
                start=True, stop=True),
                reads=[("brow",), ("or32",)], writes=psq(PSB, qc * 512, 512))
            sb.emit("dve", lambda v, qc=qc: v.tensor_copy(
                FA[:, 1536:2048], psA[:, qc * 512:qc * 512 + 512]),
                reads=psq(PSA, qc * 512, 512), writes=[("abc",)])
            sb.emit("dve", lambda v, qc=qc: v.tensor_copy(
                FA[:, 2048:2560], psB[:, qc * 512:qc * 512 + 512]),
                reads=psq(PSB, qc * 512, 512), writes=[("bbc",)])
            for dc in range(NDC):
                sb.emit("dve", lambda v, dc=dc, qc=qc: v.tensor_tensor(
                    AR[:, ST1:ST1 + 512],
                    AR[:, A_KT + dc * 2048 + qc * 512: A_KT + dc * 2048 + qc * 512 + 512],
                    FA[:, 1536:2048], op=AluOpType.mult),
                    reads=arb(A_KT + dc * 2048 + qc * 512, 512) + [("abc",)],
                    writes=arb(ST1, 512))
                sb.emit("dve", lambda v, dc=dc, qc=qc: v.tensor_tensor(
                    XT[:, dc * S + qc * 512: dc * S + qc * 512 + 512],
                    AR[:, ST1:ST1 + 512], FA[:, 2048:2560], op=AluOpType.add),
                    reads=arb(ST1, 512) + [("bbc",)],
                    writes=[xtr(dc, qc)] + [("xtf", dc, qc * 4 + j) for j in range(4)])

    def router_phase(s):
        for tc in range(NTC):
            def mmr(t, tc=tc):
                for dc in range(NDC):
                    i = t.matmul(psA[:, 1024:1032],
                                 XT[:, dc * S + tc * 128: dc * S + tc * 128 + 128],
                                 AR[:, RTR + dc * 8: RTR + dc * 8 + 8],
                                 start=(dc == 0), stop=(dc == NDC - 1))
                return i
            sb.emit("pe", mmr,
                    reads=[("xtf", dc, tc) for dc in range(NDC)] + arb(RTR, 64),
                    writes=psq(PSA, 1024, 512))
            sb.emit("dve", lambda v: v.reduce_max(rtc[:, 0:1], psA[:, 1024:1032], axis=AX.X),
                    reads=psq(PSA, 1024, 512), writes=[("rm",)])
            sb.emit("dve", lambda v: v.tensor_scalar(
                rtc[:, 1:2], rtc[:, 0:1], -1.0, None, op0=AluOpType.mult),
                reads=[("rm",)], writes=[("rmn",)])
            sb.emit("act", lambda a: a.activation(
                rt8, psA[:, 1024:1032], AF.Exp, bias=rtc[:, 1:2], scale=1.0),
                reads=psq(PSA, 1024, 512) + [("rmn",)], writes=[("re",)])
            sb.emit("dve", lambda v: v.reduce_sum(rtc[:, 2:3], rt8, axis=AX.X),
                    reads=[("re",)], writes=[("rs",)])
            sb.emit("dve", lambda v: v.reciprocal(rtc[:, 3:4], rtc[:, 2:3]),
                    reads=[("rs",)], writes=[("rsi",)])
            sb.emit("dve", lambda v: v.tensor_scalar(
                rt1, rt8, rtc[:, 3:4], None, op0=AluOpType.mult),
                reads=[("re",), ("rsi",)], writes=[("rp",)])
            sb.emit("dve", lambda v: v.reduce_max(rtc[:, 0:1], rt1, axis=AX.X),
                    reads=[("rp",)], writes=[("ra1",)])
            sb.emit("dve", lambda v: v.tensor_scalar(
                rt2, rt1, rtc[:, 0:1], None, op0=AluOpType.is_ge),
                reads=[("rp",), ("ra1",)], writes=[("rmask1",)])
            sb.emit("dve", lambda v: v.scalar_tensor_tensor(
                rt8, rt2, -1e9, rt1, op0=AluOpType.mult, op1=AluOpType.add),
                reads=[("rmask1",), ("rp",)], writes=[("rq",)])
            sb.emit("dve", lambda v: v.reduce_max(rtc[:, 1:2], rt8, axis=AX.X),
                    reads=[("rq",)], writes=[("rb",)])
            sb.emit("dve", lambda v: v.tensor_tensor(
                rtc[:, 2:3], rtc[:, 0:1], rtc[:, 1:2], op=AluOpType.add),
                reads=[("ra1",), ("rb",)], writes=[("rs2",)])
            sb.emit("act", lambda a: a.activation(
                rt8, rt1, AF.Exp, bias=rtc[:, 2:3], scale=-2.0),
                reads=[("rp",), ("rs2",)], writes=[("rexp",)])
            sb.emit("dve", lambda v: v.tensor_scalar(
                rt2, rt8, 1.0, None, op0=AluOpType.add),
                reads=[("rexp",)], writes=[("rd",)])
            sb.emit("dve", lambda v: v.reciprocal(rt8, rt2),
                    reads=[("rd",)], writes=[("rsg",)])
            sb.emit("dve", lambda v: v.tensor_scalar(
                rt2, rt1, rtc[:, 1:2], None, op0=AluOpType.is_ge),
                reads=[("rp",), ("rb",)], writes=[("rmask2",)])
            sb.emit("dve", lambda v, tc=tc: v.tensor_tensor(
                wfull[:, tc * 8:(tc + 1) * 8], rt8, rt2, op=AluOpType.mult),
                reads=[("rsg",), ("rmask2",)], writes=[("wf", tc)])

    def ffn_phase(s):
        for tc in range(NTC):
            sb.emit("dve", lambda v, tc=tc: v.memset(
                AR[:, A_R4 + tc * 1024: A_R4 + tc * 1024 + 1024], 0.0),
                writes=arb(A_R4 + tc * 1024, 1024))
        for e in range(E):
            slot = [0]

            def stream_w(dram_row0, dst_col):
                for hlf in range(2):
                    stg = (slot[0] % 3) * 1024
                    slot[0] += 1
                    sb.emit("sync", lambda q, hlf=hlf, stg=stg: q.dma_start(
                        FA[:, stg:stg + 1024],
                        wrow_src[0][dram_row0:dram_row0 + 128,
                                    hlf * 1024:hlf * 1024 + 1024]),
                        writes=fab(stg, 1024), dma=True)
                    sb.emit("pool", lambda g, hlf=hlf, stg=stg, dst_col=dst_col: g.tensor_copy(
                        AR[:, dst_col + hlf * 1024: dst_col + hlf * 1024 + 1024],
                        FA[:, stg:stg + 1024]),
                        reads=fab(stg, 1024),
                        writes=arb(dst_col + hlf * 1024, 1024))

            wrow_src = [wg_d]
            for dc in range(NDC):
                stream_w(e * DM + dc * 128, A_QT + dc * 2048)
            wrow_src = [wu_d]
            for dc in range(NDC):
                stream_w(e * DM + dc * 128, A_KT + dc * 2048)
            wdr = wd_d.rearrange("(r p) n -> p r n", p=128)
            for fc2 in range(8):
                for hlf in range(2):
                    stg = (slot[0] % 3) * 1024
                    slot[0] += 1
                    rbase = (e * FF + fc2 * 256) // 128 + hlf
                    sb.emit("sync", lambda q, rbase=rbase, stg=stg: q.dma_start(
                        FA[:, stg:stg + 1024],
                        wdr[:, rbase:rbase + 1, :]),
                        writes=fab(stg, 1024), dma=True)
                    sb.emit("pool", lambda g, fc2=fc2, hlf=hlf, stg=stg: g.tensor_copy(
                        AR[:, A_VT + fc2 * 2048 + hlf * 1024: A_VT + fc2 * 2048 + hlf * 1024 + 1024],
                        FA[:, stg:stg + 1024]),
                        reads=fab(stg, 1024),
                        writes=arb(A_VT + fc2 * 2048 + hlf * 1024, 1024))
            wg_all = arb(A_QT, 16384)
            wu_all = arb(A_KT, 16384)
            wd_all = arb(A_VT, 16384)
            for qc in range(4):
                xt_res = [xtr(dc, qc) for dc in range(NDC)]
                for fc in range(NFC):
                    pg = (fc % 2) * 512

                    def mmgate(t, qc=qc, fc=fc, pg=pg):
                        for dc in range(NDC):
                            i = t.matmul(
                                psA[:, pg:pg + 512],
                                AR[:, A_QT + dc * 2048 + fc * 128: A_QT + dc * 2048 + fc * 128 + 128],
                                XT[:, dc * S + qc * 512: dc * S + qc * 512 + 512],
                                start=(dc == 0), stop=(dc == NDC - 1))
                        return i
                    sb.emit("pe", mmgate, reads=xt_res + wg_all, writes=psq(PSA, pg, 512))
                    scol = SILU + (fc % 2) * 512
                    sb.emit("act", lambda a, pg=pg, scol=scol: a.activation(
                        AR[:, scol:scol + 512], psA[:, pg:pg + 512], AF.Silu,
                        bias=0.0, scale=1.0),
                        reads=psq(PSA, pg, 512), writes=[("sl", scol)])

                    def mmup(t, qc=qc, fc=fc, pg=pg):
                        for dc in range(NDC):
                            i = t.matmul(
                                psA[:, 1024 + pg:1024 + pg + 512],
                                AR[:, A_KT + dc * 2048 + fc * 128: A_KT + dc * 2048 + fc * 128 + 128],
                                XT[:, dc * S + qc * 512: dc * S + qc * 512 + 512],
                                start=(dc == 0), stop=(dc == NDC - 1))
                        return i
                    sb.emit("pe", mmup, reads=xt_res + wu_all,
                            writes=psq(PSA, 1024 + pg, 512))
                    fuse_col = FUSE + fc * 512
                    sb.emit("dve", lambda v, pg=pg, scol=scol, fuse_col=fuse_col: v.tensor_tensor(
                        AR[:, fuse_col:fuse_col + 512], AR[:, scol:scol + 512],
                        psA[:, 1024 + pg:1024 + pg + 512], op=AluOpType.mult),
                        reads=[("sl", scol)] + psq(PSA, 1024 + pg, 512),
                        writes=arb(fuse_col, 512))
                fuse_all = arb(FUSE, 8192)
                for t2 in range(4):
                    tc = qc * 4 + t2
                    for dh in range(2):
                        pg = 1024 + dh * 512

                        def mmdown(t, t2=t2, dh=dh, pg=pg):
                            for fc in range(NFC):
                                i = t.matmul(
                                    psB[:, pg:pg + 512],
                                    AR[:, FUSE + fc * 512 + t2 * 128: FUSE + fc * 512 + t2 * 128 + 128],
                                    AR[:, A_VT + fc * 1024 + dh * 512: A_VT + fc * 1024 + dh * 512 + 512],
                                    start=(fc == 0), stop=(fc == NFC - 1))
                            return i
                        sb.emit("pe", mmdown, reads=fuse_all + wd_all,
                                writes=psq(PSB, pg, 512))
                        ycol = A_R4 + tc * 1024 + dh * 512
                        sb.emit("dve", lambda v, tc=tc, pg=pg, e=e, ycol=ycol: v.scalar_tensor_tensor(
                            AR[:, ycol:ycol + 512], psB[:, pg:pg + 512],
                            wfull[:, tc * 8 + e: tc * 8 + e + 1],
                            AR[:, ycol:ycol + 512],
                            op0=AluOpType.mult, op1=AluOpType.add),
                            reads=psq(PSB, pg, 512) + [("wf", tc)] + arb(ycol, 512),
                            writes=arb(ycol, 512))

    def ln2_store(s):
        for tc in range(NTC):
            for dc in range(NDC):
                pcol = (dc % 2) * 128
                sb.emit("dve", lambda v, dc=dc, tc=tc: v.tensor_copy(
                    FA[:, 0:128], XT[:, dc * S + tc * 128: dc * S + tc * 128 + 128]),
                    reads=[("xtf", dc, tc)], writes=fab(0, 128))
                sb.emit("pe", lambda t, pcol=pcol: t.transpose(
                    psA[:, pcol:pcol + 128], FA[:, 0:128], id32),
                    reads=fab(0, 128) + [("id32",)], writes=psq(PSA, pcol, 128))
                sb.emit("dve", lambda v, dc=dc, tc=tc, pcol=pcol: v.tensor_tensor(
                    FA[:, 1024 + dc * 128:1024 + dc * 128 + 128],
                    psA[:, pcol:pcol + 128],
                    AR[:, A_R4 + tc * 1024 + dc * 128: A_R4 + tc * 1024 + dc * 128 + 128],
                    op=AluOpType.add),
                    reads=psq(PSA, pcol, 128) + arb(A_R4 + tc * 1024 + dc * 128, 128),
                    writes=fab(1024 + dc * 128, 128))
            sb.emit("dve", lambda v: v.reduce_sum(stat[:, 0:1], FA[:, 1024:2048], axis=AX.X),
                    reads=fab(1024, 1024), writes=[("l2s1",)])
            sb.emit("act", lambda a: a.activation(
                AR[:, ST0:ST0 + 1024], FA[:, 1024:2048], AF.Square, bias=0.0, scale=1.0),
                reads=fab(1024, 1024), writes=arb(ST0, 1024))
            sb.emit("dve", lambda v: v.reduce_sum(
                stat[:, 1:2], AR[:, ST0:ST0 + 1024], axis=AX.X),
                reads=arb(ST0, 1024), writes=[("l2s2",)])
            sb.emit("dve", lambda v: v.tensor_scalar(
                stat[:, 2:3], stat[:, 0:1], 1.0 / DM, None, op0=AluOpType.mult),
                reads=[("l2s1",)], writes=[("l2m",)])
            sb.emit("dve", lambda v: v.tensor_tensor(
                stat[:, 3:4], stat[:, 2:3], stat[:, 2:3], op=AluOpType.mult),
                reads=[("l2m",)], writes=[("l2m2",)])
            sb.emit("dve", lambda v: v.scalar_tensor_tensor(
                stat[:, 4:5], stat[:, 1:2], 1.0 / DM, stat[:, 3:4],
                op0=AluOpType.mult, op1=AluOpType.subtract),
                reads=[("l2s2",), ("l2m2",)], writes=[("l2var",)])
            sb.emit("act", lambda a: a.activation(
                stat[:, 5:6], stat[:, 4:5], AF.Ln, bias=bvec[:, 2:3], scale=1.0),
                reads=[("l2var",), ("bv", 2)], writes=[("l2ln",)])
            sb.emit("act", lambda a: a.activation(
                stat[:, 6:7], stat[:, 5:6], AF.Exp, bias=0.0, scale=-0.5),
                reads=[("l2ln",)], writes=[("l2rs",)])
            sb.emit("dve", lambda v: v.tensor_scalar(
                stat[:, 7:8], stat[:, 2:3], -1.0, None, op0=AluOpType.mult),
                reads=[("l2m",)], writes=[("l2mn",)])
            sb.emit("dve", lambda v: v.tensor_scalar(
                FA[:, 2048:3072], FA[:, 1024:2048], stat[:, 7:8], stat[:, 6:7],
                op0=AluOpType.add, op1=AluOpType.mult),
                reads=fab(1024, 1024) + [("l2mn",), ("l2rs",)],
                writes=fab(2048, 1024))
            sb.emit("sync", lambda q, tc=tc, s=s: q.dma_start(
                o2[s * S + tc * 128: s * S + (tc + 1) * 128, :], FA[:, 2048:3072]),
                reads=fab(2048, 1024), writes=[("store", s, tc)], dma=True)

    def dump_region(dst_ap, region, width_each, n, res_of):
        for i in range(n):
            sb.emit("dve", lambda v, i=i: v.tensor_copy(
                FA[:, 0:width_each],
                AR[:, region + i * width_each: region + (i + 1) * width_each]),
                reads=res_of(i), writes=fab(0, width_each))
            sb.emit("sync", lambda q, i=i: q.dma_start(
                dst_ap[i], FA[:, 0:width_each]),
                reads=fab(0, width_each), writes=[("dmp",)], dma=True)

    nseq = 1 if debug == 8 else B_CORE
    for s in range(nseq):
        with nc.named_scope(f"loadx_{s}"):
            load_xt(s)
            if bmask & 1:
                sb.barrier()
        if pstop == 1:
            break
        with nc.named_scope(f"qkv_{s}"):
            qkv_phase(s)
            if bmask & 2:
                sb.barrier()
        if debug and debug != 8 and s == 0:
            dump_region(dbg["q"], A_QT, 2048, NDC, lambda i: arb(A_QT + i * 2048, 2048))
            dump_region(dbg["k"], A_KT, 2048, NDC, lambda i: arb(A_KT + i * 2048, 2048))
            dump_region(dbg["v"], A_VT, 1024, NTC, lambda i: arb(A_VT + i * 1024, 1024))
            if debug == 2:
                break
        if pstop == 2:
            break
        with nc.named_scope(f"attn_{s}"):
            attn_phase(s)
            if bmask & 4:
                sb.barrier()
        if debug and debug != 8 and s == 0:
            dump_region(dbg["attnT"], A_R4, 2048, NDC, lambda i: arb(A_R4 + i * 2048, 2048))
            if debug in (3, 31):
                break
        if pstop == 3:
            break
        with nc.named_scope(f"oproj_{s}"):
            outproj_ln1(s)
            if bmask & 8:
                sb.barrier()
        if debug and debug != 8 and s == 0:
            for dc in range(NDC):
                sb.emit("dve", lambda v, dc=dc: v.tensor_copy(
                    FA[:, 0:2048], XT[:, dc * S:(dc + 1) * S]),
                    reads=[xtr(dc, q_) for q_ in range(4)], writes=fab(0, 2048))
                sb.emit("sync", lambda q, dc=dc: q.dma_start(
                    dbg["x1"][dc], FA[:, 0:2048]),
                    reads=fab(0, 2048), writes=[("dmp",)], dma=True)
            if debug == 4:
                break
        if pstop == 4:
            break
        with nc.named_scope(f"router_{s}"):
            router_phase(s)
            if bmask & 16:
                sb.barrier()
        if debug and debug != 8 and s == 0:
            for tc in range(NTC):
                sb.emit("sync", lambda q, tc=tc: q.dma_start(
                    dbg["wfull"][tc], wfull[:, tc * 8:(tc + 1) * 8]),
                    reads=[("wf", tc)], writes=[("dmp",)], dma=True)
            if debug == 5:
                break
        if pstop == 5:
            break
        with nc.named_scope(f"ffn_{s}"):
            ffn_phase(s)
            if bmask & 32:
                sb.barrier()
        if debug and debug != 8 and s == 0:
            dump_region(dbg["y"], A_R4, 1024, NTC, lambda i: arb(A_R4 + i * 1024, 1024))
            if debug == 6:
                break
        if pstop == 6:
            break
        with nc.named_scope(f"ln2_{s}"):
            ln2_store(s)
            if bmask & 64:
                sb.barrier()
        if debug and debug != 8:
            break
    nc.sync.wait_ge(sb.sem["sync"], sb.cnt["sync"])
    return nc


def kernel(L0_attn_x, L0_attn_qkv_w, L0_attn_qkv_b, L0_attn_out_w, L0_attn_out_b,
           L0_ffn_router_kernel_DE, L0_ffn_kernel_gating_EDF, L0_ffn_kernel_up_EDF,
           L0_ffn_kernel_down_EFD, _debug=0):
    x = np.ascontiguousarray(np.asarray(L0_attn_x, dtype=np.float32))
    common = {
        "qkv_w": np.ascontiguousarray(np.asarray(L0_attn_qkv_w, np.float32)),
        "qkv_b": np.ascontiguousarray(np.asarray(L0_attn_qkv_b, np.float32)),
        "out_w": np.ascontiguousarray(np.asarray(L0_attn_out_w, np.float32)),
        "out_b": np.ascontiguousarray(np.asarray(L0_attn_out_b, np.float32)),
        "router": np.ascontiguousarray(np.asarray(L0_ffn_router_kernel_DE, np.float32)),
        "w_gate": np.ascontiguousarray(
            np.asarray(L0_ffn_kernel_gating_EDF, np.float32).reshape(E * DM, FF)),
        "w_up": np.ascontiguousarray(
            np.asarray(L0_ffn_kernel_up_EDF, np.float32).reshape(E * DM, FF)),
        "w_down": np.ascontiguousarray(
            np.asarray(L0_ffn_kernel_down_EFD, np.float32).reshape(E * FF, DM)),
    }
    nc = build_program(debug=_debug, bmask=127)
    in_maps = []
    for c in range(NCORES):
        m = dict(common)
        m["x"] = x[c * B_CORE:(c + 1) * B_CORE]
        in_maps.append(m)
    res = run_bass_kernel_spmd(nc, in_maps, core_ids=list(range(NCORES)))
    out = np.concatenate([np.asarray(r["out"], np.float32) for r in res.results], axis=0)
    if _debug:
        return out, res.results
    return out


# version 05
# speedup vs baseline: 1.2454x; 1.0181x over previous
import numpy as np
import concourse.bass as bass
import concourse.mybir as mybir
from concourse.bass_utils import run_bass_kernel_spmd
from concourse.alu_op_type import AluOpType

F32 = mybir.dt.float32
BF16 = mybir.dt.bfloat16
I32 = mybir.dt.int32
AF = mybir.ActivationFunctionType
AX = mybir.AxisListType

B_FULL, S, DM = 16, 2048, 1024
NCORES = 8
B_CORE = B_FULL // NCORES
H, HD = 16, 64
E, FF = 8, 2048
NTC = S // 128
NDC = DM // 128
NFC = FF // 128

G1, LA1, G2, PA, PB, PC, PBIAS = (9.33208063, -6.83677266, 0.31989422,
                                  -16.53699443, 30.24374182, 170.18486467,
                                  -249.23997529)
SSHIFT = 40.0

A_QT = 0
A_KT = 16384
A_VT = 32768
A_R4 = 49152
A_SP = 73728
AR_COLS = 81920

ST0 = A_SP
ST1 = A_SP + 2048
BIAS512 = A_SP + 4096
QHAT = A_SP + 4608
SILU = A_SP + 5120
COLV = A_SP + 6144
RTR = A_SP + 6400
FUSE = A_R4 + 16384

FA_COLS = 3072


class SB:

    ENGS = ("pe", "act", "dve", "pool", "sync")

    def __init__(self, nc):
        self.nc = nc
        self.h = {"pe": nc.tensor, "act": nc.scalar, "dve": nc.vector,
                  "pool": nc.gpsimd, "sync": nc.sync}
        self.sem = {e: nc.alloc_semaphore(name=f"sb_{e}") for e in self.ENGS}
        self.cnt = {e: 0 for e in self.ENGS}
        self.sat = {c: {p: 0 for p in self.ENGS} for c in self.ENGS}
        self.res = {}

    def barrier(self):
        for c in self.ENGS:
            h = self.h[c]
            for p in self.ENGS:
                if p == c or self.cnt[p] == 0:
                    continue
                if self.cnt[p] > self.sat[c][p]:
                    h.wait_ge(self.sem[p], self.cnt[p])
                    self.sat[c][p] = self.cnt[p]

    def emit(self, engine, fn, reads=(), writes=(), dma=False):
        deps = {}

        def dep(eng, cntv):
            if cntv > deps.get(eng, 0):
                deps[eng] = cntv

        for r in reads:
            e = self.res.get(r)
            if e and e["w"]:
                dep(*e["w"])
        for w in writes:
            e = self.res.get(w)
            if e:
                if e["w"]:
                    dep(*e["w"])
                for pr in e["r"]:
                    dep(*pr)
        hh = self.h[engine]
        for peng in self.ENGS:
            pcnt = deps.get(peng, 0)
            if pcnt > self.sat[engine][peng]:
                hh.wait_ge(self.sem[peng], pcnt)
                self.sat[engine][peng] = pcnt
        inst = fn(hh)
        inst.then_inc(self.sem[engine], 16 if dma else 1)
        self.cnt[engine] += 16 if dma else 1
        my = (engine, self.cnt[engine])
        for r in reads:
            self.res.setdefault(r, {"w": None, "r": []})["r"].append(my)
        for w in writes:
            self.res[w] = {"w": my, "r": []}
        return inst


def build_program(debug=0, bmask=0, pstop=0):
    nc = bass.Bass("TRN2", target_bir_lowering=False)

    x_d = nc.dram_tensor("x", [B_CORE, S, DM], F32, kind="ExternalInput").ap()
    qkvw_d = nc.dram_tensor("qkv_w", [3 * DM, DM], F32, kind="ExternalInput").ap()
    qkvb_d = nc.dram_tensor("qkv_b", [3 * DM], F32, kind="ExternalInput").ap()
    outw_d = nc.dram_tensor("out_w", [DM, DM], F32, kind="ExternalInput").ap()
    outb_d = nc.dram_tensor("out_b", [DM], F32, kind="ExternalInput").ap()
    rtr_d = nc.dram_tensor("router", [DM, E], F32, kind="ExternalInput").ap()
    wg_d = nc.dram_tensor("w_gate", [E * DM, FF], F32, kind="ExternalInput").ap()
    wu_d = nc.dram_tensor("w_up", [E * DM, FF], F32, kind="ExternalInput").ap()
    wd_d = nc.dram_tensor("w_down", [E * FF, DM], F32, kind="ExternalInput").ap()
    out_d = nc.dram_tensor("out", [B_CORE, S, DM], F32, kind="ExternalOutput").ap()
    x2 = x_d.rearrange("b s d -> (b s) d")
    o2 = out_d.rearrange("b s d -> (b s) d")

    dbg = {}
    if debug:
        for nm, shp in [("q", [NDC, 128, S]), ("k", [NDC, 128, S]),
                        ("v", [NTC, 128, DM]), ("attnT", [NDC, 128, S]),
                        ("x1", [NDC, 128, S]), ("wfull", [NTC, 128, E]),
                        ("y", [NTC, 128, DM]),
                        ("st", [NTC, 128, S]), ("cs", [NTC, 128, 2]),
                        ("u", [1, 128, S]), ("svout", [1, 128, S]),
                        ("dena", [1, 128, S]), ("colv", [NTC, 128, 65]), ("colx", [NTC, 128, 4])]:
            dbg[nm] = nc.dram_tensor("dbg_" + nm, shp, F32, kind="ExternalOutput").ap()

    AR = nc.alloc_sbuf_tensor("AR", [128, AR_COLS], BF16).ap()
    FA = nc.alloc_sbuf_tensor("FA", [128, FA_COLS], F32).ap()
    XT = nc.alloc_sbuf_tensor("XT", [128, NDC * S], BF16).ap()
    id32 = nc.alloc_sbuf_tensor("id32", [128, 128], F32).ap()
    id16 = nc.alloc_sbuf_tensor("id16", [128, 128], BF16).ap()
    io_a = nc.alloc_sbuf_tensor("io_a", [128, 128], I32).ap()
    io_b = nc.alloc_sbuf_tensor("io_b", [128, 128], I32).ap()
    ones_col16 = nc.alloc_sbuf_tensor("ones_col16", [128, 1], BF16).ap()
    ones_row32 = nc.alloc_sbuf_tensor("ones_row32", [1, 128], F32).ap()
    ones64 = nc.alloc_sbuf_tensor("ones64", [65, 64], F32).ap()
    bvec = nc.alloc_sbuf_tensor("bvec", [128, 8], F32).ap()
    outb_sb = nc.alloc_sbuf_tensor("outb_sb", [128, NDC], F32).ap()
    wfull = nc.alloc_sbuf_tensor("wfull", [128, NTC * E], F32).ap()
    rt8 = nc.alloc_sbuf_tensor("rt8", [128, 8], F32).ap()
    rt1 = nc.alloc_sbuf_tensor("rt1", [128, 8], F32).ap()
    rt2 = nc.alloc_sbuf_tensor("rt2", [128, 8], F32).ap()
    rtc = nc.alloc_sbuf_tensor("rtc", [128, 6], F32).ap()
    ss8 = nc.alloc_sbuf_tensor("ss8", [128, 16], F32).ap()
    cs2 = nc.alloc_sbuf_tensor("cs2", [128, 4], F32).ap()
    csx = nc.alloc_sbuf_tensor("csx", [128, 4], F32).ap()
    stat = nc.alloc_sbuf_tensor("stat", [128, 16], F32).ap()

    psA = nc.alloc_psum_tensor("psA", [128, 2048], F32).ap()
    psB = nc.alloc_psum_tensor("psB", [128, 2048], F32).ap()

    sb = SB(nc)

    def arb(col, width):
        return [("ar", b) for b in range(col // 2048, (col + width + 2047) // 2048)]

    def fab(col, width):
        return [("fa", b) for b in range(col // 512, (col + width + 511) // 512)]

    def psq(t, col, width):
        return [(t, b) for b in range(col // 512, (col + width + 511) // 512)]

    PSA, PSB = "psA", "psB"
    xtr = lambda dc, blk: ("xt", dc, blk)

    sb.emit("pool", lambda g: g.iota(io_a, [[1, 128]], base=0, channel_multiplier=0),
            writes=[("io", 0)])
    sb.emit("pool", lambda g: g.iota(io_b, [[0, 128]], base=0, channel_multiplier=1),
            writes=[("io", 1)])
    sb.emit("dve", lambda v: v.tensor_tensor(id32, io_b, io_a, op=AluOpType.is_equal),
            reads=[("io", 0), ("io", 1)], writes=[("id32",)])
    sb.emit("dve", lambda v: v.tensor_copy(id16, id32),
            reads=[("id32",)], writes=[("id16",)])
    sb.emit("dve", lambda v: v.memset(ones_col16, 1.0), writes=[("oc16",)])
    sb.emit("dve", lambda v: v.memset(ones_row32, 1.0), writes=[("or32",)])
    sb.emit("dve", lambda v: v.memset(ones64, 1.0), writes=[("o64",)])
    for i, val in [(0, PBIAS + SSHIFT), (1, LA1), (2, 1e-5), (3, 1e-20), (4, PB)]:
        sb.emit("dve", lambda v, i=i, val=val: v.memset(bvec[:, i:i + 1], val),
                writes=[("bv", i)])
    sb.emit("sync", lambda q: q.dma_start(
        outb_sb, outb_d.rearrange("(c p) -> p c", p=128),
        allow_slow_non_contiguous=True), writes=[("outb",)], dma=True)
    for dc in range(NDC):
        sb.emit("sync", lambda q, dc=dc: q.dma_start(
            FA[:, dc * 8:dc * 8 + 8], rtr_d[dc * 128:(dc + 1) * 128, :]),
            writes=fab(dc * 8, 8), dma=True)
    sb.emit("pool", lambda g: g.tensor_copy(AR[:, RTR:RTR + 64], FA[:, 0:64]),
            reads=fab(0, 64), writes=arb(RTR, 64))

    def load_xt(s):
        for tc in range(NTC):
            stg = (tc % 2) * 1024
            sb.emit("sync", lambda q, tc=tc, stg=stg: q.dma_start(
                FA[:, stg:stg + 1024], x2[s * S + tc * 128: s * S + (tc + 1) * 128, :]),
                writes=fab(stg, 1024), dma=True)
            for dc in range(NDC):
                pcol = (dc % 2) * 128
                sb.emit("pe", lambda t, dc=dc, stg=stg, pcol=pcol: t.transpose(
                    psB[:, pcol:pcol + 128], FA[:, stg + dc * 128:stg + dc * 128 + 128], id32),
                    reads=fab(stg + dc * 128, 128) + [("id32",)],
                    writes=psq(PSB, pcol, 128))
                sb.emit("dve", lambda v, dc=dc, tc=tc, pcol=pcol: v.tensor_copy(
                    XT[:, dc * S + tc * 128: dc * S + tc * 128 + 128], psB[:, pcol:pcol + 128]),
                    reads=psq(PSB, pcol, 128), writes=[xtr(dc, tc // 4), ("xtf", dc, tc)])

    def load_transpose_w(dram, rows, wcol_base, chunk_cols):
        for rc in range(rows // 128):
            stg = 2048
            sb.emit("sync", lambda q, rc=rc: q.dma_start(
                FA[:, stg:stg + 1024], dram[rc * 128:(rc + 1) * 128, :]),
                writes=fab(stg, 1024), dma=True)
            for dc in range(NDC):
                pcol = 256 + (dc % 2) * 128
                sb.emit("pe", lambda t, dc=dc, pcol=pcol: t.transpose(
                    psB[:, pcol:pcol + 128], FA[:, stg + dc * 128:stg + dc * 128 + 128], id32),
                    reads=fab(stg + dc * 128, 128) + [("id32",)],
                    writes=psq(PSB, pcol, 128))
                dst = wcol_base + dc * chunk_cols + rc * 128
                sb.emit("dve", lambda v, dst=dst, pcol=pcol: v.tensor_copy(
                    AR[:, dst:dst + 128], psB[:, pcol:pcol + 128]),
                    reads=psq(PSB, pcol, 128), writes=arb(dst, 128))

    def qkv_phase(s):
        load_transpose_w(qkvw_d, 3 * DM, A_R4, 3072)
        wqkv_res = arb(A_R4, 24576)
        for fch in range(6):
            fbase = fch * 512
            sb.emit("sync", lambda q, fbase=fbase: q.dma_start(
                FA[0:1, 1536:2048], qkvb_d[fbase:fbase + 512].rearrange("(o n) -> o n", o=1)),
                writes=[("qbrow",)], dma=True)
            sb.emit("pe", lambda t: t.matmul(
                psB[:, 1536:2048], ones_row32, FA[0:1, 1536:2048], start=True, stop=True),
                reads=[("qbrow",), ("or32",)], writes=psq(PSB, 1536, 512))
            sb.emit("dve", lambda v: v.tensor_copy(FA[:, 1024:1536], psB[:, 1536:2048]),
                    reads=psq(PSB, 1536, 512), writes=fab(1024, 512))
            for tc in range(NTC):
                pq = (tc % 2) * 512

                def mmg(t, tc=tc, fbase=fbase, pq=pq):
                    for dc in range(NDC):
                        i = t.matmul(psA[:, pq:pq + 512],
                                     XT[:, dc * S + tc * 128: dc * S + tc * 128 + 128],
                                     AR[:, A_R4 + dc * 3072 + fbase: A_R4 + dc * 3072 + fbase + 512],
                                     start=(dc == 0), stop=(dc == NDC - 1))
                    return i
                sb.emit("pe", mmg,
                        reads=[("xtf", dc, tc) for dc in range(NDC)] + wqkv_res,
                        writes=psq(PSA, pq, 512))
                sb.emit("dve", lambda v, pq=pq: v.tensor_tensor(
                    FA[:, 0:512], psA[:, pq:pq + 512], FA[:, 1024:1536], op=AluOpType.add),
                    reads=psq(PSA, pq, 512) + fab(1024, 512), writes=fab(0, 512))
                if fch >= 4:
                    vcol = A_VT + tc * 1024 + (fch - 4) * 512
                    sb.emit("dve", lambda v, vcol=vcol: v.tensor_copy(
                        AR[:, vcol:vcol + 512], FA[:, 0:512]),
                        reads=fab(0, 512), writes=arb(vcol, 512))
                else:
                    for hh in range(8):
                        sb.emit("dve", lambda v, hh=hh: v.scalar_tensor_tensor(
                            FA[:, 512 + hh * 64:512 + hh * 64 + 64],
                            FA[:, hh * 64:hh * 64 + 64], 1.0,
                            FA[:, hh * 64:hh * 64 + 64],
                            op0=AluOpType.mult, op1=AluOpType.mult,
                            accum_out=ss8[:, hh:hh + 1]),
                            reads=fab(0, 512), writes=fab(512, 64) + [("ss", hh)])
                    sb.emit("act", lambda a: a.activation(
                        ss8[:, 8:16], ss8[:, 0:8], AF.Ln, bias=bvec[:, 3:4], scale=1.0),
                        reads=[("ss", hh) for hh in range(8)] + [("bv", 3)],
                        writes=[("ssl",)])
                    sb.emit("act", lambda a: a.activation(
                        ss8[:, 0:8], ss8[:, 8:16], AF.Exp, bias=0.0, scale=-0.5),
                        reads=[("ssl",)], writes=[("rn",)] + [("ss", hh) for hh in range(8)])
                    for hh in range(8):
                        sb.emit("dve", lambda v, hh=hh: v.tensor_scalar(
                            FA[:, 512 + hh * 64:512 + hh * 64 + 64],
                            FA[:, hh * 64:hh * 64 + 64],
                            ss8[:, hh:hh + 1], None, op0=AluOpType.mult),
                            reads=fab(0, 512) + [("rn",)] + fab(512 + hh * 64, 64),
                            writes=[("qhf", hh)])
                    base_region = A_QT if fch < 2 else A_KT
                    for pp in range(4):
                        chunk = (fch % 2) * 4 + pp
                        pcol = 1024 + (pp % 4) * 128
                        sb.emit("pe", lambda t, pp=pp, pcol=pcol: t.transpose(
                            psA[:, pcol:pcol + 128],
                            FA[:, 512 + pp * 128:512 + pp * 128 + 128], id32),
                            reads=[("qhf", 2 * pp), ("qhf", 2 * pp + 1), ("id32",)],
                            writes=psq(PSA, pcol, 128))
                        dst = base_region + chunk * 2048 + tc * 128
                        sb.emit("dve", lambda v, dst=dst, pcol=pcol: v.tensor_copy(
                            AR[:, dst:dst + 128], psA[:, pcol:pcol + 128]),
                            reads=psq(PSA, pcol, 128), writes=arb(dst, 128))

    def attn_phase(s):
        for head in range(H):
            chunk, prow = head // 2, (head % 2) * 64
            qt_res = arb(A_QT + chunk * 2048, 2048)
            kt_res = arb(A_KT + chunk * 2048, 2048)
            def make_expf(par, stc, q4):
                def go():
                    sb.emit("act", lambda a: a.activation(
                        AR[:, stc + q4 * 512:stc + q4 * 512 + 512], FA[:, par:par + 512],
                        AF.Exp, bias=bvec[:, 0:1], scale=1.0,
                        accum_out=cs2[:, q4:q4 + 1]),
                        reads=fab(par, 512) + [("bv", 0)],
                        writes=arb(stc + q4 * 512, 512) + [("cs", q4)])
                return go

            def kc_final(kc, stc, head=head, chunk=chunk, prow=prow):
                sb.emit("act", lambda a: a.activation(
                    ss8[:, 12:16], cs2[:, 0:4], AF.Copy, bias=0.0, scale=1.0,
                    accum_out=csx[:, 0:1]),
                    reads=[("cs", i) for i in range(4)], writes=[("css",), ("cssj",)])
                sb.emit("act", lambda a: a.activation(
                    csx[:, 1:2], csx[:, 0:1], AF.Ln, bias=0.0, scale=1.0),
                    reads=[("css",)], writes=[("csl",)])
                sb.emit("act", lambda a: a.activation(
                    csx[:, 2:3], csx[:, 1:2], AF.Exp, bias=0.0, scale=-0.5),
                    reads=[("csl",)], writes=[("col",)])
                cvc = COLV + (kc % 2) * 65
                sb.emit("dve", lambda v, kc=kc, head=head, cvc=cvc: v.tensor_scalar(
                    AR[:, cvc:cvc + 64],
                    AR[:, A_VT + kc * 1024 + head * 64: A_VT + kc * 1024 + head * 64 + 64],
                    csx[:, 2:3], None, op0=AluOpType.mult),
                    reads=[("col",)] + arb(A_VT + kc * 1024 + head * 64, 64),
                    writes=[("colv", kc % 2)])
                sb.emit("dve", lambda v, cvc=cvc: v.tensor_copy(
                    AR[:, cvc + 64:cvc + 65], csx[:, 2:3]),
                    reads=[("col",)], writes=[("colvd", kc % 2)])

                def mmsv(t, kc=kc, stc=stc, cvc=cvc):
                    for qc in range(4):
                        i = t.matmul(psB[0:65, qc * 512:(qc + 1) * 512],
                                     AR[:, cvc:cvc + 65],
                                     AR[:, stc + qc * 512:stc + qc * 512 + 512],
                                     start=(kc == 0), stop=(kc == NTC - 1))
                    return i
                sb.emit("pe", mmsv,
                        reads=[("colv", kc % 2), ("colvd", kc % 2)] + arb(stc, 2048),
                        writes=psq(PSB, 0, 2048))

            pend = None
            for kc in range(NTC):
                stc = ST0 if kc % 2 == 0 else ST1
                for q4 in range(4):
                    par = ((kc * 4 + q4) % 2) * 1536
                    pq = q4 * 512
                    sb.emit("pe", lambda t, kc=kc, chunk=chunk, prow=prow, pq=pq: t.matmul(
                        psA[:, pq:pq + 512],
                        AR[prow:prow + 64,
                           A_KT + chunk * 2048 + kc * 128: A_KT + chunk * 2048 + kc * 128 + 128],
                        AR[prow:prow + 64,
                           A_QT + chunk * 2048 + pq: A_QT + chunk * 2048 + pq + 512],
                        start=True, stop=True),
                        reads=qt_res + kt_res, writes=psq(PSA, pq, 512))
                    sb.emit("act", lambda a, par=par, pq=pq: a.activation(
                        FA[:, par:par + 512], psA[:, pq:pq + 512], AF.Exp,
                        bias=bvec[:, 1:2], scale=G1),
                        reads=psq(PSA, pq, 512) + [("bv", 1)], writes=fab(par, 512))
                    sb.emit("act", lambda a, par=par, pq=pq: a.activation(
                        FA[:, par + 512:par + 1024], psA[:, pq:pq + 512], AF.Exp,
                        bias=0.0, scale=G2),
                        reads=psq(PSA, pq, 512), writes=fab(par + 512, 512))
                    sb.emit("act", lambda a, par=par: a.activation(
                        FA[:, par + 1024:par + 1536], FA[:, par + 512:par + 1024],
                        AF.Square, bias=bvec[:, 4:5], scale=PA),
                        reads=fab(par + 512, 512) + [("bv", 4)],
                        writes=fab(par + 1024, 512))
                    sb.emit("dve", lambda v, par=par: v.tensor_tensor(
                        FA[:, par + 512:par + 1024], FA[:, par:par + 512],
                        FA[:, par + 1024:par + 1536], op=AluOpType.add),
                        reads=fab(par, 512) + fab(par + 1024, 512),
                        writes=fab(par + 512, 512))
                    sb.emit("dve", lambda v, par=par, pq=pq: v.scalar_tensor_tensor(
                        FA[:, par:par + 512], psA[:, pq:pq + 512], PC,
                        FA[:, par + 512:par + 1024],
                        op0=AluOpType.mult, op1=AluOpType.add),
                        reads=psq(PSA, pq, 512) + fab(par + 512, 512),
                        writes=fab(par, 512))
                    if pend is not None:
                        pend()
                    pend = make_expf(par, stc, q4)
                    if q4 == 0 and kc > 0:
                        kc_final(kc - 1, ST0 if (kc - 1) % 2 == 0 else ST1)
            pend()
            pend = None
            kc_final(NTC - 1, ST0 if (NTC - 1) % 2 == 0 else ST1)
            sb.emit("dve", lambda v: v.reciprocal(FA[64:65, 1024:3072], psB[64:65, 0:2048]),
                    reads=psq(PSB, 0, 2048) + fab(1024, 2048), writes=[("denr",)])
            if debug == 31 and head == 0:
                sb.emit("sync", lambda q: q.dma_start(dbg["dena"][0], FA[:, 1024:3072]),
                        reads=[("denr",)] + fab(1024, 2048), writes=[("dmp",)], dma=True)
            for qc in range(4):
                pcol = 1024 + (qc % 2) * 512
                fcol = (qc % 2) * 512
                sb.emit("pe", lambda t, qc=qc, pcol=pcol: t.matmul(
                    psA[0:64, pcol:pcol + 512], ones64[64:65, :],
                    FA[64:65, 1024 + qc * 512:1024 + qc * 512 + 512], start=True, stop=True),
                    reads=[("denr",), ("o64",)], writes=psq(PSA, pcol, 512))
                sb.emit("dve", lambda v, pcol=pcol, fcol=fcol: v.tensor_copy(
                    FA[0:64, fcol:fcol + 512], psA[0:64, pcol:pcol + 512]),
                    reads=psq(PSA, pcol, 512), writes=[("dbc", fcol)] + fab(fcol, 512))
                dst = A_R4 + chunk * 2048 + qc * 512
                sb.emit("dve", lambda v, qc=qc, dst=dst, prow=prow, fcol=fcol: v.tensor_tensor(
                    AR[prow:prow + 64, dst:dst + 512], psB[0:64, qc * 512:qc * 512 + 512],
                    FA[0:64, fcol:fcol + 512], op=AluOpType.mult),
                    reads=psq(PSB, qc * 512, 512) + [("dbc", fcol)] + fab(fcol, 512),
                    writes=arb(dst, 512))

    def outproj_ln1(s):
        load_transpose_w(outw_d, DM, A_QT, 1024)
        wout_res = arb(A_QT, 8192)
        at_res = arb(A_R4, 16384)
        for dcp in range(NDC):
            for qc in range(4):
                pq = (qc % 2) * 512

                def mmo(t, dcp=dcp, qc=qc, pq=pq):
                    for dc in range(NDC):
                        i = t.matmul(
                            psA[:, pq:pq + 512],
                            AR[:, A_QT + dc * 1024 + dcp * 128: A_QT + dc * 1024 + dcp * 128 + 128],
                            AR[:, A_R4 + dc * 2048 + qc * 512: A_R4 + dc * 2048 + qc * 512 + 512],
                            start=(dc == 0), stop=(dc == NDC - 1))
                    return i
                sb.emit("pe", mmo, reads=wout_res + at_res, writes=psq(PSA, pq, 512))
                sb.emit("dve", lambda v, dcp=dcp, qc=qc, pq=pq: v.scalar_tensor_tensor(
                    AR[:, A_KT + dcp * 2048 + qc * 512: A_KT + dcp * 2048 + qc * 512 + 512],
                    psA[:, pq:pq + 512], outb_sb[:, dcp:dcp + 1],
                    XT[:, dcp * S + qc * 512: dcp * S + qc * 512 + 512],
                    op0=AluOpType.add, op1=AluOpType.add),
                    reads=psq(PSA, pq, 512) + [("outb",), xtr(dcp, qc)],
                    writes=arb(A_KT + dcp * 2048 + qc * 512, 512))
        for qc in range(4):
            for dc in range(NDC):
                sqc = ST0 + 512 * (dc % 4)
                sb.emit("act", lambda a, dc=dc, qc=qc, sqc=sqc: a.activation(
                    AR[:, sqc:sqc + 512],
                    AR[:, A_KT + dc * 2048 + qc * 512: A_KT + dc * 2048 + qc * 512 + 512],
                    AF.Square, bias=0.0, scale=1.0),
                    reads=arb(A_KT + dc * 2048 + qc * 512, 512), writes=arb(sqc, 512))
                sb.emit("pe", lambda t, dc=dc, qc=qc: t.matmul(
                    psA[0:1, qc * 512:qc * 512 + 512], ones_col16,
                    AR[:, A_KT + dc * 2048 + qc * 512: A_KT + dc * 2048 + qc * 512 + 512],
                    start=(dc == 0), stop=(dc == NDC - 1)),
                    reads=arb(A_KT + dc * 2048 + qc * 512, 512) + [("oc16",)],
                    writes=psq(PSA, qc * 512, 512))
                sb.emit("pe", lambda t, dc=dc, qc=qc, sqc=sqc: t.matmul(
                    psB[0:1, qc * 512:qc * 512 + 512], ones_col16,
                    AR[:, sqc:sqc + 512],
                    start=(dc == 0), stop=(dc == NDC - 1)),
                    reads=arb(sqc, 512) + [("oc16",)],
                    writes=psq(PSB, qc * 512, 512))
        for qc in range(4):
            sb.emit("dve", lambda v, qc=qc: v.tensor_scalar(
                FA[0:1, 0:512], psA[0:1, qc * 512:qc * 512 + 512], 1.0 / DM, None,
                op0=AluOpType.mult),
                reads=psq(PSA, qc * 512, 512), writes=[("mrow",)])
            sb.emit("dve", lambda v, qc=qc: v.tensor_scalar(
                FA[0:1, 512:1024], psB[0:1, qc * 512:qc * 512 + 512], 1.0 / DM, None,
                op0=AluOpType.mult),
                reads=psq(PSB, qc * 512, 512), writes=[("exrow",)])
            sb.emit("dve", lambda v: v.tensor_tensor(
                FA[0:1, 1024:1536], FA[0:1, 0:512], FA[0:1, 0:512], op=AluOpType.mult),
                reads=[("mrow",)], writes=[("m2row",)])
            sb.emit("dve", lambda v: v.tensor_tensor(
                FA[0:1, 512:1024], FA[0:1, 512:1024], FA[0:1, 1024:1536],
                op=AluOpType.subtract),
                reads=[("exrow",), ("m2row",)], writes=[("varrow",)])
            sb.emit("act", lambda a: a.activation(
                FA[0:1, 1536:2048], FA[0:1, 512:1024], AF.Ln, bias=bvec[0:1, 2:3],
                scale=1.0),
                reads=[("varrow",), ("bv", 2)], writes=[("lnrow",)])
            sb.emit("act", lambda a: a.activation(
                FA[0:1, 1024:1536], FA[0:1, 1536:2048], AF.Exp, bias=0.0, scale=-0.5),
                reads=[("lnrow",)], writes=[("arow",)])
            sb.emit("dve", lambda v: v.scalar_tensor_tensor(
                FA[0:1, 0:512], FA[0:1, 0:512], -1.0, FA[0:1, 1024:1536],
                op0=AluOpType.mult, op1=AluOpType.mult),
                reads=[("mrow",), ("arow",)], writes=[("brow",)])
            sb.emit("pe", lambda t, qc=qc: t.matmul(
                psA[:, qc * 512:qc * 512 + 512], ones_row32, FA[0:1, 1024:1536],
                start=True, stop=True),
                reads=[("arow",), ("or32",)], writes=psq(PSA, qc * 512, 512))
            sb.emit("pe", lambda t, qc=qc: t.matmul(
                psB[:, qc * 512:qc * 512 + 512], ones_row32, FA[0:1, 0:512],
                start=True, stop=True),
                reads=[("brow",), ("or32",)], writes=psq(PSB, qc * 512, 512))
            sb.emit("dve", lambda v, qc=qc: v.tensor_copy(
                FA[:, 1536:2048], psA[:, qc * 512:qc * 512 + 512]),
                reads=psq(PSA, qc * 512, 512), writes=[("abc",)])
            sb.emit("dve", lambda v, qc=qc: v.tensor_copy(
                FA[:, 2048:2560], psB[:, qc * 512:qc * 512 + 512]),
                reads=psq(PSB, qc * 512, 512), writes=[("bbc",)])
            for dc in range(NDC):
                sb.emit("dve", lambda v, dc=dc, qc=qc: v.tensor_tensor(
                    AR[:, ST1:ST1 + 512],
                    AR[:, A_KT + dc * 2048 + qc * 512: A_KT + dc * 2048 + qc * 512 + 512],
                    FA[:, 1536:2048], op=AluOpType.mult),
                    reads=arb(A_KT + dc * 2048 + qc * 512, 512) + [("abc",)],
                    writes=arb(ST1, 512))
                sb.emit("dve", lambda v, dc=dc, qc=qc: v.tensor_tensor(
                    XT[:, dc * S + qc * 512: dc * S + qc * 512 + 512],
                    AR[:, ST1:ST1 + 512], FA[:, 2048:2560], op=AluOpType.add),
                    reads=arb(ST1, 512) + [("bbc",)],
                    writes=[xtr(dc, qc)] + [("xtf", dc, qc * 4 + j) for j in range(4)])

    def router_phase(s):
        for tc in range(NTC):
            def mmr(t, tc=tc):
                for dc in range(NDC):
                    i = t.matmul(psA[:, 1024:1032],
                                 XT[:, dc * S + tc * 128: dc * S + tc * 128 + 128],
                                 AR[:, RTR + dc * 8: RTR + dc * 8 + 8],
                                 start=(dc == 0), stop=(dc == NDC - 1))
                return i
            sb.emit("pe", mmr,
                    reads=[("xtf", dc, tc) for dc in range(NDC)] + arb(RTR, 64),
                    writes=psq(PSA, 1024, 512))
            sb.emit("dve", lambda v: v.reduce_max(rtc[:, 0:1], psA[:, 1024:1032], axis=AX.X),
                    reads=psq(PSA, 1024, 512), writes=[("rm",)])
            sb.emit("dve", lambda v: v.tensor_scalar(
                rtc[:, 1:2], rtc[:, 0:1], -1.0, None, op0=AluOpType.mult),
                reads=[("rm",)], writes=[("rmn",)])
            sb.emit("act", lambda a: a.activation(
                rt8, psA[:, 1024:1032], AF.Exp, bias=rtc[:, 1:2], scale=1.0),
                reads=psq(PSA, 1024, 512) + [("rmn",)], writes=[("re",)])
            sb.emit("dve", lambda v: v.reduce_sum(rtc[:, 2:3], rt8, axis=AX.X),
                    reads=[("re",)], writes=[("rs",)])
            sb.emit("dve", lambda v: v.reciprocal(rtc[:, 3:4], rtc[:, 2:3]),
                    reads=[("rs",)], writes=[("rsi",)])
            sb.emit("dve", lambda v: v.tensor_scalar(
                rt1, rt8, rtc[:, 3:4], None, op0=AluOpType.mult),
                reads=[("re",), ("rsi",)], writes=[("rp",)])
            sb.emit("dve", lambda v: v.reduce_max(rtc[:, 0:1], rt1, axis=AX.X),
                    reads=[("rp",)], writes=[("ra1",)])
            sb.emit("dve", lambda v: v.tensor_scalar(
                rt2, rt1, rtc[:, 0:1], None, op0=AluOpType.is_ge),
                reads=[("rp",), ("ra1",)], writes=[("rmask1",)])
            sb.emit("dve", lambda v: v.scalar_tensor_tensor(
                rt8, rt2, -1e9, rt1, op0=AluOpType.mult, op1=AluOpType.add),
                reads=[("rmask1",), ("rp",)], writes=[("rq",)])
            sb.emit("dve", lambda v: v.reduce_max(rtc[:, 1:2], rt8, axis=AX.X),
                    reads=[("rq",)], writes=[("rb",)])
            sb.emit("dve", lambda v: v.tensor_tensor(
                rtc[:, 2:3], rtc[:, 0:1], rtc[:, 1:2], op=AluOpType.add),
                reads=[("ra1",), ("rb",)], writes=[("rs2",)])
            sb.emit("act", lambda a: a.activation(
                rt8, rt1, AF.Exp, bias=rtc[:, 2:3], scale=-2.0),
                reads=[("rp",), ("rs2",)], writes=[("rexp",)])
            sb.emit("dve", lambda v: v.tensor_scalar(
                rt2, rt8, 1.0, None, op0=AluOpType.add),
                reads=[("rexp",)], writes=[("rd",)])
            sb.emit("dve", lambda v: v.reciprocal(rt8, rt2),
                    reads=[("rd",)], writes=[("rsg",)])
            sb.emit("dve", lambda v: v.tensor_scalar(
                rt2, rt1, rtc[:, 1:2], None, op0=AluOpType.is_ge),
                reads=[("rp",), ("rb",)], writes=[("rmask2",)])
            sb.emit("dve", lambda v, tc=tc: v.tensor_tensor(
                wfull[:, tc * 8:(tc + 1) * 8], rt8, rt2, op=AluOpType.mult),
                reads=[("rsg",), ("rmask2",)], writes=[("wf", tc)])

    def ffn_phase(s):
        for tc in range(NTC):
            sb.emit("dve", lambda v, tc=tc: v.memset(
                AR[:, A_R4 + tc * 1024: A_R4 + tc * 1024 + 1024], 0.0),
                writes=arb(A_R4 + tc * 1024, 1024))
        for e in range(E):
            slot = [0]

            def stream_w(dram_row0, dst_col):
                for hlf in range(2):
                    stg = (slot[0] % 3) * 1024
                    slot[0] += 1
                    sb.emit("sync", lambda q, hlf=hlf, stg=stg: q.dma_start(
                        FA[:, stg:stg + 1024],
                        wrow_src[0][dram_row0:dram_row0 + 128,
                                    hlf * 1024:hlf * 1024 + 1024]),
                        writes=fab(stg, 1024), dma=True)
                    sb.emit("dve", lambda v, hlf=hlf, stg=stg, dst_col=dst_col: v.tensor_copy(
                        AR[:, dst_col + hlf * 1024: dst_col + hlf * 1024 + 1024],
                        FA[:, stg:stg + 1024]),
                        reads=fab(stg, 1024),
                        writes=arb(dst_col + hlf * 1024, 1024))

            wrow_src = [wg_d]
            for dc in range(NDC):
                stream_w(e * DM + dc * 128, A_QT + dc * 2048)
            wrow_src = [wu_d]
            for dc in range(NDC):
                stream_w(e * DM + dc * 128, A_KT + dc * 2048)
            wdr = wd_d.rearrange("(r p) n -> p r n", p=128)
            for fc2 in range(8):
                for hlf in range(2):
                    stg = (slot[0] % 3) * 1024
                    slot[0] += 1
                    rbase = (e * FF + fc2 * 256) // 128 + hlf
                    sb.emit("sync", lambda q, rbase=rbase, stg=stg: q.dma_start(
                        FA[:, stg:stg + 1024],
                        wdr[:, rbase:rbase + 1, :]),
                        writes=fab(stg, 1024), dma=True)
                    sb.emit("dve", lambda v, fc2=fc2, hlf=hlf, stg=stg: v.tensor_copy(
                        AR[:, A_VT + fc2 * 2048 + hlf * 1024: A_VT + fc2 * 2048 + hlf * 1024 + 1024],
                        FA[:, stg:stg + 1024]),
                        reads=fab(stg, 1024),
                        writes=arb(A_VT + fc2 * 2048 + hlf * 1024, 1024))
            wg_all = arb(A_QT, 16384)
            wu_all = arb(A_KT, 16384)
            wd_all = arb(A_VT, 16384)
            for qc in range(4):
                xt_res = [xtr(dc, qc) for dc in range(NDC)]
                for fc in range(NFC):
                    pg = (fc % 2) * 512

                    def mmgate(t, qc=qc, fc=fc, pg=pg):
                        for dc in range(NDC):
                            i = t.matmul(
                                psA[:, pg:pg + 512],
                                AR[:, A_QT + dc * 2048 + fc * 128: A_QT + dc * 2048 + fc * 128 + 128],
                                XT[:, dc * S + qc * 512: dc * S + qc * 512 + 512],
                                start=(dc == 0), stop=(dc == NDC - 1))
                        return i
                    sb.emit("pe", mmgate, reads=xt_res + wg_all, writes=psq(PSA, pg, 512))
                    scol = SILU + (fc % 2) * 512
                    sb.emit("act", lambda a, pg=pg, scol=scol: a.activation(
                        AR[:, scol:scol + 512], psA[:, pg:pg + 512], AF.Silu,
                        bias=0.0, scale=1.0),
                        reads=psq(PSA, pg, 512), writes=[("sl", scol)])

                    def mmup(t, qc=qc, fc=fc, pg=pg):
                        for dc in range(NDC):
                            i = t.matmul(
                                psA[:, 1024 + pg:1024 + pg + 512],
                                AR[:, A_KT + dc * 2048 + fc * 128: A_KT + dc * 2048 + fc * 128 + 128],
                                XT[:, dc * S + qc * 512: dc * S + qc * 512 + 512],
                                start=(dc == 0), stop=(dc == NDC - 1))
                        return i
                    sb.emit("pe", mmup, reads=xt_res + wu_all,
                            writes=psq(PSA, 1024 + pg, 512))
                    fuse_col = FUSE + fc * 512
                    sb.emit("dve", lambda v, pg=pg, scol=scol, fuse_col=fuse_col: v.tensor_tensor(
                        AR[:, fuse_col:fuse_col + 512], AR[:, scol:scol + 512],
                        psA[:, 1024 + pg:1024 + pg + 512], op=AluOpType.mult),
                        reads=[("sl", scol)] + psq(PSA, 1024 + pg, 512),
                        writes=arb(fuse_col, 512))
                fuse_all = arb(FUSE, 8192)
                for t2 in range(4):
                    tc = qc * 4 + t2
                    for dh in range(2):
                        pg = 1024 + dh * 512

                        def mmdown(t, t2=t2, dh=dh, pg=pg):
                            for fc in range(NFC):
                                i = t.matmul(
                                    psB[:, pg:pg + 512],
                                    AR[:, FUSE + fc * 512 + t2 * 128: FUSE + fc * 512 + t2 * 128 + 128],
                                    AR[:, A_VT + fc * 1024 + dh * 512: A_VT + fc * 1024 + dh * 512 + 512],
                                    start=(fc == 0), stop=(fc == NFC - 1))
                            return i
                        sb.emit("pe", mmdown, reads=fuse_all + wd_all,
                                writes=psq(PSB, pg, 512))
                        ycol = A_R4 + tc * 1024 + dh * 512
                        sb.emit("dve", lambda v, tc=tc, pg=pg, e=e, ycol=ycol: v.scalar_tensor_tensor(
                            AR[:, ycol:ycol + 512], psB[:, pg:pg + 512],
                            wfull[:, tc * 8 + e: tc * 8 + e + 1],
                            AR[:, ycol:ycol + 512],
                            op0=AluOpType.mult, op1=AluOpType.add),
                            reads=psq(PSB, pg, 512) + [("wf", tc)] + arb(ycol, 512),
                            writes=arb(ycol, 512))

    def ln2_store(s):
        for tc in range(NTC):
            for dc in range(NDC):
                pcol = (dc % 2) * 128
                sb.emit("dve", lambda v, dc=dc, tc=tc: v.tensor_copy(
                    FA[:, 0:128], XT[:, dc * S + tc * 128: dc * S + tc * 128 + 128]),
                    reads=[("xtf", dc, tc)], writes=fab(0, 128))
                sb.emit("pe", lambda t, pcol=pcol: t.transpose(
                    psA[:, pcol:pcol + 128], FA[:, 0:128], id32),
                    reads=fab(0, 128) + [("id32",)], writes=psq(PSA, pcol, 128))
                sb.emit("dve", lambda v, dc=dc, tc=tc, pcol=pcol: v.tensor_tensor(
                    FA[:, 1024 + dc * 128:1024 + dc * 128 + 128],
                    psA[:, pcol:pcol + 128],
                    AR[:, A_R4 + tc * 1024 + dc * 128: A_R4 + tc * 1024 + dc * 128 + 128],
                    op=AluOpType.add),
                    reads=psq(PSA, pcol, 128) + arb(A_R4 + tc * 1024 + dc * 128, 128),
                    writes=fab(1024 + dc * 128, 128))
            sb.emit("dve", lambda v: v.reduce_sum(stat[:, 0:1], FA[:, 1024:2048], axis=AX.X),
                    reads=fab(1024, 1024), writes=[("l2s1",)])
            sb.emit("act", lambda a: a.activation(
                AR[:, ST0:ST0 + 1024], FA[:, 1024:2048], AF.Square, bias=0.0, scale=1.0),
                reads=fab(1024, 1024), writes=arb(ST0, 1024))
            sb.emit("dve", lambda v: v.reduce_sum(
                stat[:, 1:2], AR[:, ST0:ST0 + 1024], axis=AX.X),
                reads=arb(ST0, 1024), writes=[("l2s2",)])
            sb.emit("dve", lambda v: v.tensor_scalar(
                stat[:, 2:3], stat[:, 0:1], 1.0 / DM, None, op0=AluOpType.mult),
                reads=[("l2s1",)], writes=[("l2m",)])
            sb.emit("dve", lambda v: v.tensor_tensor(
                stat[:, 3:4], stat[:, 2:3], stat[:, 2:3], op=AluOpType.mult),
                reads=[("l2m",)], writes=[("l2m2",)])
            sb.emit("dve", lambda v: v.scalar_tensor_tensor(
                stat[:, 4:5], stat[:, 1:2], 1.0 / DM, stat[:, 3:4],
                op0=AluOpType.mult, op1=AluOpType.subtract),
                reads=[("l2s2",), ("l2m2",)], writes=[("l2var",)])
            sb.emit("act", lambda a: a.activation(
                stat[:, 5:6], stat[:, 4:5], AF.Ln, bias=bvec[:, 2:3], scale=1.0),
                reads=[("l2var",), ("bv", 2)], writes=[("l2ln",)])
            sb.emit("act", lambda a: a.activation(
                stat[:, 6:7], stat[:, 5:6], AF.Exp, bias=0.0, scale=-0.5),
                reads=[("l2ln",)], writes=[("l2rs",)])
            sb.emit("dve", lambda v: v.tensor_scalar(
                stat[:, 7:8], stat[:, 2:3], -1.0, None, op0=AluOpType.mult),
                reads=[("l2m",)], writes=[("l2mn",)])
            sb.emit("dve", lambda v: v.tensor_scalar(
                FA[:, 2048:3072], FA[:, 1024:2048], stat[:, 7:8], stat[:, 6:7],
                op0=AluOpType.add, op1=AluOpType.mult),
                reads=fab(1024, 1024) + [("l2mn",), ("l2rs",)],
                writes=fab(2048, 1024))
            sb.emit("sync", lambda q, tc=tc, s=s: q.dma_start(
                o2[s * S + tc * 128: s * S + (tc + 1) * 128, :], FA[:, 2048:3072]),
                reads=fab(2048, 1024), writes=[("store", s, tc)], dma=True)

    def dump_region(dst_ap, region, width_each, n, res_of):
        for i in range(n):
            sb.emit("dve", lambda v, i=i: v.tensor_copy(
                FA[:, 0:width_each],
                AR[:, region + i * width_each: region + (i + 1) * width_each]),
                reads=res_of(i), writes=fab(0, width_each))
            sb.emit("sync", lambda q, i=i: q.dma_start(
                dst_ap[i], FA[:, 0:width_each]),
                reads=fab(0, width_each), writes=[("dmp",)], dma=True)

    nseq = 1 if debug == 8 else B_CORE
    for s in range(nseq):
        with nc.named_scope(f"loadx_{s}"):
            load_xt(s)
            if bmask & 1:
                sb.barrier()
        if pstop == 1:
            break
        with nc.named_scope(f"qkv_{s}"):
            qkv_phase(s)
            if bmask & 2:
                sb.barrier()
        if debug and debug != 8 and s == 0:
            dump_region(dbg["q"], A_QT, 2048, NDC, lambda i: arb(A_QT + i * 2048, 2048))
            dump_region(dbg["k"], A_KT, 2048, NDC, lambda i: arb(A_KT + i * 2048, 2048))
            dump_region(dbg["v"], A_VT, 1024, NTC, lambda i: arb(A_VT + i * 1024, 1024))
            if debug == 2:
                break
        if pstop == 2:
            break
        with nc.named_scope(f"attn_{s}"):
            attn_phase(s)
            if bmask & 4:
                sb.barrier()
        if debug and debug != 8 and s == 0:
            dump_region(dbg["attnT"], A_R4, 2048, NDC, lambda i: arb(A_R4 + i * 2048, 2048))
            if debug in (3, 31):
                break
        if pstop == 3:
            break
        with nc.named_scope(f"oproj_{s}"):
            outproj_ln1(s)
            if bmask & 8:
                sb.barrier()
        if debug and debug != 8 and s == 0:
            for dc in range(NDC):
                sb.emit("dve", lambda v, dc=dc: v.tensor_copy(
                    FA[:, 0:2048], XT[:, dc * S:(dc + 1) * S]),
                    reads=[xtr(dc, q_) for q_ in range(4)], writes=fab(0, 2048))
                sb.emit("sync", lambda q, dc=dc: q.dma_start(
                    dbg["x1"][dc], FA[:, 0:2048]),
                    reads=fab(0, 2048), writes=[("dmp",)], dma=True)
            if debug == 4:
                break
        if pstop == 4:
            break
        with nc.named_scope(f"router_{s}"):
            router_phase(s)
            if bmask & 16:
                sb.barrier()
        if debug and debug != 8 and s == 0:
            for tc in range(NTC):
                sb.emit("sync", lambda q, tc=tc: q.dma_start(
                    dbg["wfull"][tc], wfull[:, tc * 8:(tc + 1) * 8]),
                    reads=[("wf", tc)], writes=[("dmp",)], dma=True)
            if debug == 5:
                break
        if pstop == 5:
            break
        with nc.named_scope(f"ffn_{s}"):
            ffn_phase(s)
            if bmask & 32:
                sb.barrier()
        if debug and debug != 8 and s == 0:
            dump_region(dbg["y"], A_R4, 1024, NTC, lambda i: arb(A_R4 + i * 1024, 1024))
            if debug == 6:
                break
        if pstop == 6:
            break
        with nc.named_scope(f"ln2_{s}"):
            ln2_store(s)
            if bmask & 64:
                sb.barrier()
        if debug and debug != 8:
            break
    nc.sync.wait_ge(sb.sem["sync"], sb.cnt["sync"])
    return nc


def kernel(L0_attn_x, L0_attn_qkv_w, L0_attn_qkv_b, L0_attn_out_w, L0_attn_out_b,
           L0_ffn_router_kernel_DE, L0_ffn_kernel_gating_EDF, L0_ffn_kernel_up_EDF,
           L0_ffn_kernel_down_EFD, _debug=0):
    x = np.ascontiguousarray(np.asarray(L0_attn_x, dtype=np.float32))
    common = {
        "qkv_w": np.ascontiguousarray(np.asarray(L0_attn_qkv_w, np.float32)),
        "qkv_b": np.ascontiguousarray(np.asarray(L0_attn_qkv_b, np.float32)),
        "out_w": np.ascontiguousarray(np.asarray(L0_attn_out_w, np.float32)),
        "out_b": np.ascontiguousarray(np.asarray(L0_attn_out_b, np.float32)),
        "router": np.ascontiguousarray(np.asarray(L0_ffn_router_kernel_DE, np.float32)),
        "w_gate": np.ascontiguousarray(
            np.asarray(L0_ffn_kernel_gating_EDF, np.float32).reshape(E * DM, FF)),
        "w_up": np.ascontiguousarray(
            np.asarray(L0_ffn_kernel_up_EDF, np.float32).reshape(E * DM, FF)),
        "w_down": np.ascontiguousarray(
            np.asarray(L0_ffn_kernel_down_EFD, np.float32).reshape(E * FF, DM)),
    }
    nc = build_program(debug=_debug, bmask=127)
    in_maps = []
    for c in range(NCORES):
        m = dict(common)
        m["x"] = x[c * B_CORE:(c + 1) * B_CORE]
        in_maps.append(m)
    res = run_bass_kernel_spmd(nc, in_maps, core_ids=list(range(NCORES)))
    out = np.concatenate([np.asarray(r["out"], np.float32) for r in res.results], axis=0)
    if _debug:
        return out, res.results
    return out
